# version 12
# baseline (speedup 1.0000x reference)
import numpy as np
import concourse.bass as bass
import concourse.mybir as mybir
from concourse.bass_utils import run_bass_kernel_spmd

F32 = mybir.dt.float32
F32R = mybir.dt.float32r
BF16 = mybir.dt.bfloat16
AF = mybir.ActivationFunctionType
ALU = mybir.AluOpType
AX = mybir.AxisListType

D = 1024
T = 2048
NCH = 8
TC = 512
NT = T // TC
NQ = T // 512
NE = 64
NEXP = NE + 2
EPS_LN = 1e-5
EG_SZ = D * 512
EXP_SZ = 3 * EG_SZ

SYNC, PE, DVE, ACT, POOL = "sync", "tensor", "vector", "scalar", "gpsimd"
CAST_GU_ENG = ACT
HS2_ALIAS = True
PIPE_IL = True
ENGS = [SYNC, PE, DVE, ACT, POOL]


class Sem:
    def __init__(self, name):
        self.name = name
        self.h = None
        self.n = 0


class Ctx:
    def __init__(self):
        self.ops = {e: [] for e in ENGS}
        self.last_wait = {e: {} for e in ENGS}

    def op(self, eng, fn, inc=None, amt=1):
        if inc is not None and amt:
            inc.n += amt
            self.ops[eng].append(lambda e, f=fn, s=inc, a=amt: f(e).then_inc(s.h, a))
            return inc.n
        self.ops[eng].append(fn)
        return None

    def wait(self, eng, sem, val):
        if val is None or val <= 0:
            return
        lw = self.last_wait[eng]
        if lw.get(sem.name, -1) >= val:
            return
        lw[sem.name] = val
        self.ops[eng].append(lambda e, s=sem, v=val: e.wait_ge(s.h, v))


def build_nc(stage="full"):
    nc = bass.Bass("TRN2", target_bir_lowering=False)

    xT_h = nc.dram_tensor("xT", [D, T], F32, kind="ExternalInput")
    wkT_h = nc.dram_tensor("wkT", [D, D], F32, kind="ExternalInput")
    wvT_h = nc.dram_tensor("wvT", [D, D], F32, kind="ExternalInput")
    wrT_h = nc.dram_tensor("wrT", [D, D], F32, kind="ExternalInput")
    woT_h = nc.dram_tensor("woT", [D, D], F32, kind="ExternalInput")
    gateT_h = nc.dram_tensor("gateT", [D, 64], F32, kind="ExternalInput")
    gbias_h = nc.dram_tensor("gbias", [1, 64], F32, kind="ExternalInput")
    vecs_h = nc.dram_tensor("vecs", [128, 40], F32, kind="ExternalInput")
    ew_h = nc.dram_tensor("EW", [NEXP * EXP_SZ], F32, kind="ExternalInput")
    out_h = nc.dram_tensor("outT", [D, T], F32, kind="ExternalOutput")
    dbg_h = nc.dram_tensor("dbg", [128, 8 * 512], F32, kind="ExternalOutput") if stage in ("attn_dbg", "ffn_dbg") else None

    BASE = 16384 + 4096
    LIMIT = 229376
    cur = [BASE]

    sb_offs = {}

    def sb(name, shape, dtype, at=None):
        nbytes = int(np.prod(shape[1:])) * mybir.dt.size(dtype)
        if at is None:
            at = cur[0]
            cur[0] = at + ((nbytes + 31) // 32) * 32
        assert at + nbytes <= LIMIT, f"SBUF overflow at {name}: {at}+{nbytes}"
        sb_offs[name] = at
        return nc.alloc_sbuf_tensor_at(name, shape, dtype, offset=at)

    x_res = sb("x_res", [128, NCH * T], F32)
    vecs = sb("vecs", [128, 40], F32)
    dv = sb("dv", [128, 48], F32)
    lam_tiles = sb("lam_tiles", [128, NCH * TC], F32)
    ones_tile = sb("ones_tile", [128, TC], F32)
    ones_col = sb("ones_col", [1, 128], F32)
    ones_red = sb("ones_red", [128, 1], F32)
    iota128 = sb("iota128", [128, 128], F32)
    pvec128 = sb("pvec128", [128, 1], F32)
    identity = sb("identity", [128, 128], F32)
    ones64 = sb("ones64", [64, 128], F32)
    pvec64 = sb("pvec64", [64, 1], F32)
    gbias_sb = sb("gbias_sb", [1, 64], F32)
    bias_bc = sb("bias_bc", [128, 64], F32)
    rows = sb("rows", [1, 512 * 4], F32)
    a_bc = sb("a_bc", [128, 512], F32)
    b_bc = sb("b_bc", [128, 512], F32)
    wstg = sb("wstg", [128, 2 * 1024], F32)
    onehot = sb("onehot", [64, 2 * 128], F32R)
    peq = sb("peq", [64, 2], F32)
    epsb = sb("epsb", [128, 1], F32)

    pa_base = cur[0]
    awb = sb("awb", [128, 4 * 1024], BF16)
    xk_all = sb("xk_all", [128, NCH * TC], BF16)
    xv_all = sb("xv_all", [128, NCH * TC], BF16)
    xr_all = sb("xr_all", [128, NCH * TC], BF16)
    rwkv_all = sb("rwkv_all", [128, NCH * TC], BF16)
    xn_prev = sb("xn_prev", [128, NCH], F32)
    bcarry = sb("bcarry", [128, NCH], F32)
    ccarry = sb("ccarry", [128, NCH], F32)
    bbuf = sb("bbuf", [128, 2 * 520], F32)
    cbuf = sb("cbuf", [128, 2 * 520], F32)
    scr = sb("scr", [128, 14 * 512], F32)
    xn_ext = sb("xn_ext", [128, 2 * 520], F32)
    a_bc4 = sb("a_bc4", [128, T], F32)
    b_bc4 = sb("b_bc4", [128, T], F32)
    pa_end = cur[0]

    cur[0] = pa_base
    X2n = sb("X2n", [128, NCH * T], BF16)
    comb = sb("comb", [128, 16 * 64], F32)
    combT = sb("combT", [64, T], F32R)
    comb_bc = sb("comb_bc", [128, T], F32)
    eg_sb = sb("eg_sb", [128, 2 * 1024], BF16)
    eu_sb = sb("eu_sb", [128, 2 * 1024], BF16)
    ed_sb = sb("ed_sb", [128, 4 * 512], BF16)
    gateT_sb = sb("gateT_sb", [128, NCH * 64], BF16)
    Hs = sb("Hs", [128, 4 * T], BF16)
    Hs2 = sb("Hs2", [128, 4 * T], BF16, at=sb_offs["lam_tiles"]) if HS2_ALIAS else Hs
    sG = sb("sG", [128, 4 * 512], F32)
    Uc = sb("Uc", [128, 2 * 512], F32)
    r_sig = sb("r_sig", [128, 64], F32)
    r_sc = sb("r_sc", [128, 64], F32)
    r_gs = sb("r_gs", [128, 8], F32)
    r_m8g = sb("r_m8g", [128, 8], F32)
    r_gm = sb("r_gm", [128, 8], F32)
    r_m64 = sb("r_m64", [128, 64], F32)
    r_m8 = sb("r_m8", [128, 8], F32)
    r_z = sb("r_z", [128, 4], F32)
    r_sel = sb("r_sel", [128, 64], F32)
    dbg = None
    if stage == "ffn_dbg":
        dbg = sb("dbg", [128, 8 * 512], F32, at=sb_offs["lam_tiles"])
    pbc_end = cur[0]
    import os
    if os.environ.get("PRINT_MAP"):
        print(f"MAP: pa_base={pa_base} pa_end={pa_end} pbc_end={pbc_end} LIMIT={LIMIT} slack_pa={LIMIT-pa_end} slack_bc={LIMIT-pbc_end}")
    assert max(pa_end, pbc_end) <= LIMIT

    pbs = [nc.psum_tensor(f"pb{i}", [128, 512], F32).__enter__() for i in range(8)]

    sems = {}
    for name in ["s_ld0", "s_lx", "s_w0", "s_w1", "s_pe", "s_dv", "s_ac", "s_po", "s_st"]:
        s = Sem(name)
        s.h = nc.semaphore(name).__enter__()
        sems[name] = s
    s_ld0, s_lx, s_pe, s_dv, s_ac, s_po, s_st = (sems[n] for n in
                                                 ["s_ld0", "s_lx", "s_pe", "s_dv", "s_ac", "s_po", "s_st"])
    s_w = [sems["s_w0"], sems["s_w1"]]

    k = Ctx()
    bank_free = {i: None for i in range(8)}

    def use_banks(ids):
        for b in ids:
            if bank_free[b] is not None:
                k.wait(PE, *bank_free[b])

    def release_bank(b, sem, val):
        bank_free[b] = (sem, val)

    def scr_ap(slot, w=512):
        return scr[:, slot * 512: slot * 512 + w]

    k.op(SYNC, lambda e: e.dma_start(out=vecs[:, :], in_=vecs_h[:, :]), s_ld0, 16)
    v_gb = k.op(SYNC, lambda e: e.dma_start(out=gbias_sb[:, :], in_=gbias_h[:, :]), s_ld0, 16)
    v_vecs = v_gb
    v_x = None
    for c in range(NCH):
        v_x = k.op(SYNC, lambda e, c=c: e.dma_start(
            out=x_res[:, c * T:(c + 1) * T], in_=xT_h[c * 128:(c + 1) * 128, :]), s_lx, 16)

    k.op(POOL, lambda e: e.iota(iota128[:, :], [[1, 128]], channel_multiplier=0,
                                allow_small_or_imprecise_dtypes=True))
    k.op(POOL, lambda e: e.iota(pvec128[:, :], [[1, 1]], channel_multiplier=1,
                                allow_small_or_imprecise_dtypes=True))
    v_s = k.op(POOL, lambda e: e.iota(pvec64[:, :], [[1, 1]], channel_multiplier=1,
                                      allow_small_or_imprecise_dtypes=True), s_po, 1)
    k.wait(POOL, s_po, v_s)
    k.op(POOL, lambda e: e.tensor_scalar(out=identity[:, :], in0=iota128[:, :],
                                         scalar1=pvec128[:, :1], scalar2=None, op0=ALU.is_equal))
    k.op(POOL, lambda e: e.memset(ones64[:, :], 1.0))
    k.op(POOL, lambda e: e.memset(ones_tile[:, :], 1.0))
    k.op(POOL, lambda e: e.memset(ones_col[:, :], 1.0))
    k.op(POOL, lambda e: e.memset(ones_red[:, :], 1.0))
    k.op(POOL, lambda e: e.memset(epsb[:, :], EPS_LN))
    k.op(POOL, lambda e: e.memset(xn_prev[:, :], 0.0))
    k.op(POOL, lambda e: e.memset(bcarry[:, :], 0.0))
    v_consts = k.op(POOL, lambda e: e.memset(ccarry[:, :], 0.0), s_po, 1)

    k.wait(ACT, s_ld0, v_vecs)
    k.op(ACT, lambda e: e.activation(out=dv[:, 0:8], in_=vecs[:, 0:8], func=AF.Exp))
    k.op(ACT, lambda e: e.activation(out=dv[:, 8:16], in_=dv[:, 0:8], func=AF.Exp, scale=-1.0))
    k.op(ACT, lambda e: e.activation(out=dv[:, 16:24], in_=vecs[:, 8:16], func=AF.Exp))
    k.wait(ACT, s_po, v_consts)
    v_lamt = None
    for c in range(NCH):
        v_lamt = k.op(ACT, lambda e, c=c: e.activation(
            out=lam_tiles[:, c * TC:(c + 1) * TC], in_=ones_tile[:, :],
            func=AF.Copy, scale=dv[:, 8 + c:9 + c]), s_ac, 1)
    k.wait(DVE, s_ld0, v_vecs)
    k.op(DVE, lambda e: e.tensor_scalar(out=dv[:, 24:32], in0=vecs[:, 16:24], scalar1=-1.0,
                                        scalar2=1.0, op0=ALU.mult, op1=ALU.add))
    k.op(DVE, lambda e: e.tensor_scalar(out=dv[:, 32:40], in0=vecs[:, 24:32], scalar1=-1.0,
                                        scalar2=1.0, op0=ALU.mult, op1=ALU.add))
    v_mix1 = k.op(DVE, lambda e: e.tensor_scalar(out=dv[:, 40:48], in0=vecs[:, 32:40],
                                                 scalar1=-1.0, scalar2=1.0, op0=ALU.mult,
                                                 op1=ALU.add), s_dv, 1)

    ln_state = {"v_last": None}

    def ln_stats(xsl, sqbufs, bsum, bss, ba, bb_, out_a=None, out_b=None):
        use_banks([bsum, bss])
        if ln_state["v_last"] is not None:
            k.wait(ACT, s_pe, ln_state["v_last"])
        v_prev = {}
        for c in range(NCH):
            if (c - 2) in v_prev:
                k.wait(ACT, s_pe, v_prev[c - 2])
            v_sq = k.op(ACT, lambda e, b=sqbufs[c % 2], xi=xsl(c): e.activation(
                out=b, in_=xi, func=AF.Square), s_ac, 1)
            k.op(PE, lambda e, c=c, xi=xsl(c): e.matmul(pbs[bsum][0:1, :], ones_red[:, :], xi,
                                             start=(c == 0), stop=(c == NCH - 1)))
            k.wait(PE, s_ac, v_sq)
            v_prev[c] = k.op(PE, lambda e, c=c, b=sqbufs[c % 2]: e.matmul(
                pbs[bss][0:1, :], ones_red[:, :], b,
                start=(c == 0), stop=(c == NCH - 1)), s_pe, 1)
        v_stats = v_prev[NCH - 1]
        ln_state["v_last"] = v_stats
        R_m = rows[0:1, 0:512]
        R_t1 = rows[0:1, 512:1024]
        R_t2 = rows[0:1, 1024:1536]
        R_a = rows[0:1, 512:1024]
        R_b = rows[0:1, 1536:2048]
        k.wait(DVE, s_pe, v_stats)
        k.op(DVE, lambda e: e.tensor_scalar(out=R_m, in0=pbs[bsum][0:1, :], scalar1=1.0 / D,
                                            scalar2=None, op0=ALU.mult))
        k.op(DVE, lambda e: e.tensor_scalar(out=R_t1, in0=pbs[bss][0:1, :], scalar1=1.0 / D,
                                            scalar2=None, op0=ALU.mult))
        k.op(DVE, lambda e: e.tensor_tensor(out=R_t2, in0=R_m, in1=R_m, op=ALU.mult))
        v_var = k.op(DVE, lambda e: e.tensor_tensor(out=R_t1, in0=R_t1, in1=R_t2,
                                                    op=ALU.subtract), s_dv, 1)
        k.wait(ACT, s_dv, v_var)
        v_sqr = k.op(ACT, lambda e: e.activation(out=R_t2, in_=R_t1, func=AF.Sqrt,
                                                 bias=epsb[0:1, :1]), s_ac, 1)
        k.wait(DVE, s_ac, v_sqr)
        v_s = k.op(DVE, lambda e: e.reciprocal(out=R_a, in_=R_t2), s_dv, 1)
        k.wait(DVE, s_dv, v_s)
        v_rows = k.op(DVE, lambda e: e.scalar_tensor_tensor(
            out=R_b, in0=R_m, scalar=-1.0, in1=R_a, op0=ALU.mult, op1=ALU.mult), s_dv, 1)
        use_banks([ba, bb_])
        k.wait(PE, s_dv, v_rows)
        k.op(PE, lambda e: e.matmul(pbs[ba][:, :], ones_col[:, :], R_a, start=True, stop=True))
        v_bcm = k.op(PE, lambda e: e.matmul(pbs[bb_][:, :], ones_col[:, :], R_b,
                                            start=True, stop=True), s_pe, 1)
        if out_a is None:
            out_a, out_b = a_bc[:, :], b_bc[:, :]
        k.wait(DVE, s_pe, v_bcm)
        k.op(DVE, lambda e, oa=out_a: e.tensor_copy(out=oa, in_=pbs[ba][:, :]))
        v_ab = k.op(DVE, lambda e, ob=out_b: e.tensor_copy(out=ob, in_=pbs[bb_][:, :]), s_dv, 1)
        for b in (bsum, bss, ba, bb_):
            release_bank(b, s_dv, v_ab)
        return v_ab

    wslot_free = [None, None]
    stg_idx = [0]

    def stage_slab(src_ap, dstbuf, dslot, width, slot_free, cast_eng=POOL):
        cast_sem = {POOL: s_po, ACT: s_ac, DVE: s_dv}[cast_eng]
        sslot = stg_idx[0] % 2
        stg_idx[0] += 1
        if wslot_free[sslot] is not None:
            k.wait(SYNC, *wslot_free[sslot])
        dst = wstg[:, sslot * 1024: sslot * 1024 + width]
        dst3 = dst.rearrange("p (a b) -> p a b", b=128)
        v_d = k.op(SYNC, lambda e, s=src_ap, o=dst3: e.dma_start(out=o, in_=s), s_w[sslot], 16)
        if slot_free[dslot] is not None:
            k.wait(cast_eng, *slot_free[dslot])
        k.wait(cast_eng, s_w[sslot], v_d)
        if cast_eng == ACT:
            v_c = k.op(ACT, lambda e, ss=sslot, dstbuf=dstbuf, dslot=dslot, width=width: e.copy(
                out=dstbuf[:, dslot * width:(dslot + 1) * width],
                in_=wstg[:, ss * 1024: ss * 1024 + width]), s_ac, 1)
        else:
            v_c = k.op(cast_eng, lambda e, ss=sslot, dstbuf=dstbuf, dslot=dslot, width=width: e.tensor_copy(
                out=dstbuf[:, dslot * width:(dslot + 1) * width],
                in_=wstg[:, ss * 1024: ss * 1024 + width]), cast_sem, 1)
        wslot_free[sslot] = (cast_sem, v_c)
        return (cast_sem, v_c)

    wdrams = {"k": wkT_h, "v": wvT_h, "r": wrT_h, "o": woT_h}
    awslot_free = [None, None, None, None]
    aw_staged = {}

    def stage_attn_slab(j):
        aslot = j % 4
        if j < 24:
            proj, ec = ("k", "v", "r")[j % 3], j // 3
        else:
            proj, ec = "o", j - 24
        src = bass.AP(wdrams[proj], ec * 128, [[D, 128], [128 * D, 8], [1, 128]])
        sem, v_c = stage_slab(src, awb, aslot, 1024, awslot_free, cast_eng=ACT)
        aw_staged[j] = (aslot, (sem, v_c))

    MIX = [("k", xk_all, 16, 24), ("v", xv_all, 24, 32), ("r", xr_all, 32, 40)]
    k.wait(ACT, s_lx, v_x)
    k.wait(PE, s_lx, v_x)
    k.wait(PE, s_po, v_consts)

    def xsl_of(tile):
        t0 = tile * TC

        def xsl(c, t0=t0):
            return x_res[:, c * T + t0: c * T + t0 + TC]
        return xsl

    for tile in range(NT):
        ln_stats(xsl_of(tile), [scr_ap(0), scr_ap(1)], 0, 1, 2, 3,
                 out_a=a_bc4[:, tile * TC:(tile + 1) * TC],
                 out_b=b_bc4[:, tile * TC:(tile + 1) * TC])
    slot_user = {}
    v_x2_tile = None
    v_allmix_of = {}
    k.wait(DVE, s_ac, v_lamt)
    k.wait(DVE, s_dv, v_mix1)

    slot_dv = {}
    v_pchunk_of = {}
    pend = []

    def emit_wkv_tail(item):
        ec, sl, pt, num, rec, v_dvchunk = item
        k.wait(ACT, s_dv, v_dvchunk)
        k.op(ACT, lambda e, rec=rec, num=num: e.activation(
            out=num, in_=rec, func=AF.Ln))
        v_rec = k.op(ACT, lambda e, num=num, rec=rec: e.activation(
            out=rec, in_=num, func=AF.Exp, scale=-1.0), s_ac, 1)
        k.wait(POOL, s_ac, v_rec)
        v_pchunk_of[ec] = k.op(POOL, lambda e, pt=pt, rec=rec, ec=ec: e.tensor_tensor(
            out=rwkv_all[:, ec * TC:(ec + 1) * TC], in0=pt, in1=rec,
            op=ALU.mult), s_po, 1)
        slot_user[sl] = (slot_dv[sl], v_pchunk_of[ec])

    def emit_mix_chunk(tile, c):
        xsl = xsl_of(tile)
        abc = a_bc4[:, tile * TC:(tile + 1) * TC]
        bbc = b_bc4[:, tile * TC:(tile + 1) * TC]
        xe = xn_ext[:, (c % 2) * 520:(c % 2) * 520 + 513]
        k.op(DVE, lambda e, xe=xe, c=c: e.tensor_copy(out=xe[:, 0:1],
                                                      in_=xn_prev[:, c:c + 1]))
        k.op(DVE, lambda e, xe=xe, xi=xsl(c), abc=abc: e.tensor_tensor(
            out=xe[:, 1:513], in0=xi, in1=abc, op=ALU.mult))
        v_s = k.op(DVE, lambda e, xe=xe, bbc=bbc: e.tensor_tensor(
            out=xe[:, 1:513], in0=xe[:, 1:513], in1=bbc, op=ALU.add), s_dv, 1)
        k.wait(DVE, s_dv, v_s)
        k.op(DVE, lambda e, xe=xe, c=c: e.tensor_copy(out=xn_prev[:, c:c + 1],
                                                      in_=xe[:, 512:513]))
        v_mx = None
        for pname, xall, mco, m1co in MIX:
            tmp = scr_ap(c % 2)
            k.op(DVE, lambda e, tmp=tmp, xe=xe, m1co=m1co, c=c: e.tensor_scalar(
                out=tmp, in0=xe[:, 0:512], scalar1=dv[:, m1co + c:m1co + c + 1],
                scalar2=None, op0=ALU.mult))
            v_mx = k.op(DVE, lambda e, xall=xall, tmp=tmp, xe=xe, mco=mco, c=c:
                        e.scalar_tensor_tensor(
                            out=xall[:, c * TC:(c + 1) * TC], in0=xe[:, 1:513],
                            scalar=vecs[:, mco + c:mco + c + 1], in1=tmp,
                            op0=ALU.mult, op1=ALU.add), s_dv, 1)
        return v_mx

    for c in range(NCH):
        v_allmix_of[0] = emit_mix_chunk(0, c)

    for tile in range(NT):
        xsl = xsl_of(tile)
        v_allmix = v_allmix_of[tile]

        pbank = [0]
        for j in range(3):
            stage_attn_slab(j)
        for ec in range(NCH):
            grp = {}
            for pi, (proj, xall) in enumerate((("k", xk_all), ("v", xv_all), ("r", xr_all))):
                b = pbank[0]
                pbank[0] = (pbank[0] + 1) % 8
                use_banks([b])
                aslot, (csem, v_c) = aw_staged[ec * 3 + pi]
                k.wait(PE, csem, v_c)
                k.wait(PE, s_dv, v_allmix)
                v_mm = None
                for kc in range(NCH):
                    v_mm = k.op(PE, lambda e, b=b, aslot=aslot, kc=kc, xall=xall,
                                s=(kc == 0), p=(kc == NCH - 1): e.matmul(
                                    pbs[b][:, :],
                                    awb[:, aslot * 1024 + kc * 128: aslot * 1024 + kc * 128 + 128],
                                    xall[:, kc * TC:(kc + 1) * TC], start=s, stop=p),
                                s_pe, 1 if kc == NCH - 1 else None)
                grp[proj] = (b, v_mm)
                awslot_free[aslot] = (s_pe, v_mm)
            sl = ec % 2
            ek, vv, er = scr_ap(2 + sl), scr_ap(4 + sl), scr_ap(6 + sl)
            pt, num, rec = scr_ap(8 + sl), scr_ap(10 + sl), scr_ap(12 + sl)
            den = vv
            if sl in slot_user:
                dve_snap, pool_snap = slot_user[sl]
                k.wait(ACT, s_dv, dve_snap)
                k.wait(ACT, s_po, pool_snap)
            bK, vK = grp["k"]
            bV, vV = grp["v"]
            bR, vR = grp["r"]
            k.wait(ACT, s_pe, vK)
            v_ek = k.op(ACT, lambda e, bK=bK, ek=ek: e.activation(
                out=ek, in_=pbs[bK][:, :], func=AF.Exp), s_ac, 1)
            release_bank(bK, s_ac, v_ek)
            k.wait(ACT, s_pe, vR)
            v_er = k.op(ACT, lambda e, bR=bR, er=er: e.activation(
                out=er, in_=pbs[bR][:, :], func=AF.Exp), s_ac, 1)
            release_bank(bR, s_ac, v_er)
            k.wait(ACT, s_pe, vV)
            v_vv = k.op(ACT, lambda e, bV=bV, vv=vv: e.copy(out=vv, in_=pbs[bV][:, :]), s_ac, 1)
            release_bank(bV, s_ac, v_vv)
            if sl in slot_user:
                k.wait(POOL, s_dv, slot_user[sl][0])
            k.wait(POOL, s_ac, v_vv)
            v_p = k.op(POOL, lambda e, ek=ek, vv=vv, pt=pt: e.tensor_tensor(
                out=pt, in0=ek, in1=vv, op=ALU.mult), s_po, 1)
            bb = bbuf[:, sl * 520: sl * 520 + 513]
            cb = cbuf[:, sl * 520: sl * 520 + 513]
            lamt = lam_tiles[:, ec * TC:(ec + 1) * TC]
            k.wait(DVE, s_po, v_p)
            k.wait(DVE, s_ac, v_vv)
            v_s = k.op(DVE, lambda e, bb=bb, ec=ec: e.tensor_copy(
                out=bb[:, 0:1], in_=bcarry[:, ec:ec + 1]), s_dv, 1)
            v_s = k.op(DVE, lambda e, cb=cb, ec=ec: e.tensor_copy(
                out=cb[:, 0:1], in_=ccarry[:, ec:ec + 1]), s_dv, 1)
            k.wait(DVE, s_dv, v_s)
            k.op(DVE, lambda e, bb=bb, lamt=lamt, ek=ek: e.tensor_tensor_scan(
                out=bb[:, 1:513], data0=lamt, data1=ek, initial=bb[:, 0:1],
                op0=ALU.mult, op1=ALU.add))
            v_s = k.op(DVE, lambda e, cb=cb, lamt=lamt, pt=pt: e.tensor_tensor_scan(
                out=cb[:, 1:513], data0=lamt, data1=pt, initial=cb[:, 0:1],
                op0=ALU.mult, op1=ALU.add), s_dv, 1)
            k.wait(DVE, s_dv, v_s)
            k.op(DVE, lambda e, bb=bb, ec=ec: e.tensor_copy(out=bcarry[:, ec:ec + 1],
                                                            in_=bb[:, 512:513]))
            k.op(DVE, lambda e, cb=cb, ec=ec: e.tensor_copy(out=ccarry[:, ec:ec + 1],
                                                            in_=cb[:, 512:513]))
            k.op(DVE, lambda e, den=den, ek=ek, bb=bb, ec=ec: e.scalar_tensor_tensor(
                out=den, in0=ek, scalar=dv[:, 16 + ec:17 + ec], in1=bb[:, 0:TC],
                op0=ALU.mult, op1=ALU.add))
            k.op(DVE, lambda e, num=num, pt=pt, cb=cb, ec=ec: e.scalar_tensor_tensor(
                out=num, in0=pt, scalar=dv[:, 16 + ec:17 + ec], in1=cb[:, 0:TC],
                op0=ALU.mult, op1=ALU.add))
            k.op(DVE, lambda e, den=den, er=er, rec=rec: e.tensor_tensor(
                out=rec, in0=den, in1=er, op=ALU.mult))
            k.op(DVE, lambda e, den=den, rec=rec: e.tensor_tensor(
                out=rec, in0=rec, in1=den, op=ALU.add))
            v_dvchunk = k.op(DVE, lambda e, num=num, er=er, pt=pt: e.tensor_tensor(
                out=pt, in0=num, in1=er, op=ALU.mult), s_dv, 1)
            for j in range(ec * 3 + 3, min(ec * 3 + 6, 32)):
                stage_attn_slab(j)
            pend.append((ec, sl, pt, num, rec, v_dvchunk))
            slot_dv[sl] = v_dvchunk
            if len(pend) > 1:
                emit_wkv_tail(pend.pop(0))
        emit_wkv_tail(pend.pop(0))
        v_rw_all = v_pchunk_of[NCH - 1]

        for ec in range(NCH):
            aslot, (csem, v_c) = aw_staged[24 + ec]
            b = pbank[0]
            pbank[0] = (pbank[0] + 1) % 8
            use_banks([b])
            k.wait(PE, csem, v_c)
            k.wait(PE, s_po, v_rw_all)
            v_mm = None
            for kc in range(NCH):
                v_mm = k.op(PE, lambda e, b=b, aslot=aslot, kc=kc,
                            s=(kc == 0), p=(kc == NCH - 1): e.matmul(
                                pbs[b][:, :],
                                awb[:, aslot * 1024 + kc * 128: aslot * 1024 + kc * 128 + 128],
                                rwkv_all[:, kc * TC:(kc + 1) * TC], start=s, stop=p),
                            s_pe, 1 if kc == NCH - 1 else None)
            awslot_free[aslot] = (s_pe, v_mm)
            if 24 + ec + 3 < 32:
                stage_attn_slab(24 + ec + 3)
            if tile + 1 < NT:
                v_allmix_of[tile + 1] = emit_mix_chunk(tile + 1, ec)
            k.wait(DVE, s_pe, v_mm)
            v_x2_tile = k.op(DVE, lambda e, b=b, xi=xsl(ec): e.tensor_tensor(
                out=xi, in0=xi, in1=pbs[b][:, :], op=ALU.add), s_dv, 1)
            release_bank(b, s_dv, v_x2_tile)

    if stage == "attn":
        v_st = None
        for c in range(NCH):
            k.wait(SYNC, s_dv, v_x2_tile)
            v_st = k.op(SYNC, lambda e, c=c: e.dma_start(
                out=out_h[c * 128:(c + 1) * 128, :], in_=x_res[:, c * T:(c + 1) * T]), s_st, 16)
        k.wait(SYNC, s_st, v_st)
        _play(nc, k)
        return nc

    v_gate_cast = None
    for kc in range(NCH):
        sslot = stg_idx[0] % 2
        stg_idx[0] += 1
        if wslot_free[sslot] is not None:
            k.wait(SYNC, *wslot_free[sslot])
        v_d = k.op(SYNC, lambda e, kc=kc, ss=sslot: e.dma_start(
            out=wstg[:, ss * 1024: ss * 1024 + 64],
            in_=gateT_h[kc * 128:(kc + 1) * 128, :]), s_w[sslot], 16)
        k.wait(POOL, s_w[sslot], v_d)
        v_gate_cast = k.op(POOL, lambda e, kc=kc, ss=sslot: e.tensor_copy(
            out=gateT_sb[:, kc * 64:(kc + 1) * 64],
            in_=wstg[:, ss * 1024: ss * 1024 + 64]), s_po, 1)
        wslot_free[sslot] = (s_po, v_gate_cast)

    use_banks([0])
    k.wait(PE, s_ld0, v_gb)
    k.wait(PE, s_po, v_consts)
    v_bb = k.op(PE, lambda e: e.matmul(pbs[0][:, 0:64], ones_col[:, :], gbias_sb[:, :],
                                       start=True, stop=True), s_pe, 1)
    k.wait(DVE, s_pe, v_bb)
    v_bbc = k.op(DVE, lambda e: e.tensor_copy(out=bias_bc[:, :], in_=pbs[0][:, 0:64]), s_dv, 1)
    release_bank(0, s_dv, v_bbc)

    k.wait(ACT, s_dv, v_x2_tile)
    k.wait(PE, s_dv, v_x2_tile)
    v_x2n_all = None
    for q in range(NQ):
        t0 = q * 512

        def x2sl(c, t0=t0):
            return x_res[:, c * T + t0: c * T + t0 + 512]

        sqb = [sG[:, 0:512], sG[:, 512:1024]]
        v_ab2 = ln_stats(x2sl, sqb, 1, 2, 3, 4)
        for c in range(NCH):
            y = Uc[:, (c % 2) * 512:(c % 2) * 512 + 512]
            k.op(DVE, lambda e, y=y, xi=x2sl(c): e.tensor_tensor(out=y, in0=xi, in1=a_bc[:, :],
                                                                 op=ALU.mult))
            v_x2n_all = k.op(DVE, lambda e, c=c, y=y, t0=t0: e.tensor_tensor(
                out=X2n[:, c * T + t0: c * T + t0 + 512], in0=y, in1=b_bc[:, :],
                op=ALU.add), s_dv, 1)

    v_comb = None
    gb_bank = [5]
    for tc in range(16):
        b = gb_bank[0]
        gb_bank[0] = 5 + ((gb_bank[0] - 5 + 1) % 3)
        use_banks([b])
        k.wait(PE, s_po, v_gate_cast)
        k.wait(PE, s_dv, v_x2n_all)
        v_mm = None
        for kc in range(NCH):
            v_mm = k.op(PE, lambda e, b=b, kc=kc, tc=tc, s=(kc == 0), p=(kc == NCH - 1):
                        e.matmul(pbs[b][:, 0:64],
                                 X2n[:, kc * T + tc * 128: kc * T + tc * 128 + 128],
                                 gateT_sb[:, kc * 64:(kc + 1) * 64], start=s, stop=p),
                        s_pe, 1 if kc == NCH - 1 else None)
        if v_comb is not None:
            k.wait(ACT, s_dv, v_comb)
        k.wait(ACT, s_pe, v_mm)
        v_sig = k.op(ACT, lambda e, b=b: e.activation(out=r_sig[:, :], in_=pbs[b][:, 0:64],
                                                      func=AF.Sigmoid), s_ac, 1)
        release_bank(b, s_ac, v_sig)
        k.wait(DVE, s_ac, v_sig)
        k.wait(DVE, s_dv, v_bbc)
        k.op(DVE, lambda e: e.tensor_tensor(out=r_sc[:, :], in0=r_sig[:, :], in1=bias_bc[:, :],
                                            op=ALU.add))
        v_s = k.op(DVE, lambda e: e.tensor_reduce(
            out=r_gs[:, :].rearrange("p (g o) -> p g o", g=8),
            in_=r_sc[:, :].rearrange("p (g ee) -> p g ee", g=8), axis=AX.X, op=ALU.max), s_dv, 1)
        k.wait(DVE, s_dv, v_s)
        v_s = k.op(DVE, lambda e: e.max(out=r_m8g[:, :], in_=r_gs[:, :]), s_dv, 1)
        k.wait(DVE, s_dv, v_s)
        v_s = k.op(DVE, lambda e: e.tensor_scalar(out=r_gm[:, :], in0=r_gs[:, :],
                                                   scalar1=r_m8g[:, 3:4], scalar2=None,
                                                   op0=ALU.is_ge), s_dv, 1)
        k.wait(DVE, s_dv, v_s)
        for g in range(8):
            v_s = k.op(DVE, lambda e, g=g: e.tensor_scalar(
                out=r_m64[:, g * 8:(g + 1) * 8], in0=r_sc[:, g * 8:(g + 1) * 8],
                scalar1=r_gm[:, g:g + 1], scalar2=None, op0=ALU.mult), s_dv, 1)
        k.wait(DVE, s_dv, v_s)
        v_s = k.op(DVE, lambda e: e.max(out=r_m8[:, :], in_=r_m64[:, :]), s_dv, 1)
        k.wait(DVE, s_dv, v_s)
        v_s = k.op(DVE, lambda e: e.tensor_reduce(out=r_z[:, 0:1], in_=r_m8[:, 0:6], axis=AX.X,
                                                  op=ALU.add), s_dv, 1)
        k.wait(DVE, s_dv, v_s)
        v_s = k.op(DVE, lambda e: e.tensor_scalar(out=r_z[:, 1:2], in0=r_z[:, 0:1], scalar1=1e-8,
                                                   scalar2=None, op0=ALU.add), s_dv, 1)
        k.wait(DVE, s_dv, v_s)
        v_s = k.op(DVE, lambda e: e.reciprocal(out=r_z[:, 2:3], in_=r_z[:, 1:2]), s_dv, 1)
        k.wait(DVE, s_dv, v_s)
        v_s = k.op(DVE, lambda e: e.tensor_scalar(out=r_sel[:, :], in0=r_m64[:, :],
                                                   scalar1=r_m8[:, 5:6], scalar2=None,
                                                   op0=ALU.is_ge), s_dv, 1)
        k.wait(DVE, s_dv, v_s)
        v_comb = k.op(DVE, lambda e, tc=tc: e.scalar_tensor_tensor(
            out=comb[:, tc * 64:(tc + 1) * 64], in0=r_m64[:, :], scalar=r_z[:, 2:3],
            in1=r_sel[:, :], op0=ALU.mult, op1=ALU.mult), s_dv, 1)
        if stage == "ffn_dbg" and tc == 0:
            k.op(DVE, lambda e: e.tensor_copy(out=dbg[:, 5 * 512:5 * 512 + 64], in_=r_sc[:, :]))
            k.op(DVE, lambda e: e.tensor_copy(out=dbg[:, 6 * 512:6 * 512 + 64], in_=r_sel[:, :]))
            k.op(DVE, lambda e: e.tensor_copy(out=dbg[:, 7 * 512:7 * 512 + 8], in_=r_m8[:, :]))
            k.op(DVE, lambda e: e.tensor_copy(out=dbg[:, 7 * 512 + 8:7 * 512 + 12], in_=r_z[:, :]))
            k.op(DVE, lambda e: e.tensor_copy(out=dbg[:, 7 * 512 + 16:7 * 512 + 24], in_=r_gs[:, :]))
            k.op(DVE, lambda e: e.tensor_copy(out=dbg[:, 7 * 512 + 24:7 * 512 + 32], in_=r_m8g[:, :]))
            k.op(DVE, lambda e: e.tensor_copy(out=dbg[:, 7 * 512 + 32:7 * 512 + 40], in_=r_gm[:, :]))
            k.op(DVE, lambda e: e.tensor_copy(out=dbg[:, 6 * 512 + 64:6 * 512 + 128], in_=r_m64[:, :]))

    v_combT = None
    for tc in range(16):
        b = 5 + (tc % 3)
        use_banks([b])
        k.wait(PE, s_dv, v_comb)
        k.wait(PE, s_po, v_consts)
        v_tp = k.op(PE, lambda e, b=b, tc=tc: e.transpose(
            pbs[b][0:64, 0:128], comb[:, tc * 64:(tc + 1) * 64], identity[:, :]), s_pe, 1)
        k.wait(DVE, s_pe, v_tp)
        k.op(DVE, lambda e, b=b: e.tensor_copy(
            out=sG[0:64, 0:128], in_=pbs[b][0:64, 0:128]))
        v_combT = k.op(DVE, lambda e, tc=tc: e.tensor_copy(
            out=combT[:, tc * 128:(tc + 1) * 128], in_=sG[0:64, 0:128]), s_dv, 1)
        release_bank(b, s_dv, v_combT)

    if stage == "ffn_dbg":
        k.op(DVE, lambda e: e.tensor_copy(out=dbg[:, 0:512], in_=X2n[:, 0:512]))
        k.op(DVE, lambda e: e.tensor_copy(out=dbg[:, 512:1536], in_=comb[:, :]))

    if stage == "ln2":
        v_st = None
        for c in range(NCH):
            k.wait(SYNC, s_dv, v_combT)
            v_st = k.op(SYNC, lambda e, c=c: e.dma_start(
                out=out_h[c * 128:(c + 1) * 128, :], in_=x_res[:, c * T:(c + 1) * T]), s_st, 16)
        k.wait(SYNC, s_st, v_st)
        _play(nc, k)
        return nc

    egslot_free = [None, None]
    euslot_free = [None, None]
    edslot_free = [None, None, None, None]

    def exp_slab_src(e_idx, kind, idx):
        base = e_idx * EXP_SZ + {"eg": 0, "eu": EG_SZ, "ed": 2 * EG_SZ}[kind]
        if kind in ("eg", "eu"):
            return bass.AP(ew_h, base + idx * 128, [[512, 128], [128 * 512, 8], [1, 128]])
        return bass.AP(ew_h, base + idx * 128, [[1024, 128], [128 * 1024, 4], [1, 128]])

    sguc_free = [None, None, None, None]
    v_out_final = {}
    hsbufs = [Hs, Hs2]
    gpar = [0]

    def next_banks():
        b = [0, 1, 2, 3] if gpar[0] == 0 else [4, 5, 6, 7]
        gpar[0] ^= 1
        use_banks(b)
        return b

    v_hs_of = {}
    v_cbc = None
    v_hs = None
    v_oh_of = {}
    staged = {}

    def stage_subiter(ej, ic):
        if ej > NEXP or ic > 3:
            return
        if ej < NEXP:
            staged[(ej, ic, "eg")] = stage_slab(exp_slab_src(ej, "eg", ic), eg_sb,
                                                ic % 2, 1024, egslot_free,
                                                cast_eng=CAST_GU_ENG)
            staged[(ej, ic, "eu")] = stage_slab(exp_slab_src(ej, "eu", ic), eu_sb,
                                                ic % 2, 1024, euslot_free,
                                                cast_eng=CAST_GU_ENG)
        if ej >= 1:
            staged[(ej, ic, "ed0")] = stage_slab(exp_slab_src(ej - 1, "ed", 2 * ic),
                                                 ed_sb, (ic % 2) * 2, 512, edslot_free,
                                                 cast_eng=POOL)
            staged[(ej, ic, "ed1")] = stage_slab(exp_slab_src(ej - 1, "ed", 2 * ic + 1),
                                                 ed_sb, (ic % 2) * 2 + 1, 512, edslot_free,
                                                 cast_eng=POOL)

    stage_subiter(0, 0)

    def emit_onehot(ej):
        oslot = ej % 2
        v_s = k.op(DVE, lambda e, oslot=oslot, ee=float(ej - 2): e.tensor_scalar(
            out=peq[:, oslot:oslot + 1], in0=pvec64[:, :], scalar1=ee, scalar2=None,
            op0=ALU.is_equal), s_dv, 1)
        k.wait(DVE, s_dv, v_s)
        v_oh_of[ej] = k.op(DVE, lambda e, oslot=oslot: e.tensor_scalar(
            out=onehot[:, oslot * 128:(oslot + 1) * 128], in0=ones64[:, :],
            scalar1=peq[:, oslot:oslot + 1], scalar2=None, op0=ALU.mult), s_dv, 1)

    emit_onehot(2)

    for ei in range(NEXP + 1):
        prod = ei < NEXP
        cons = ei >= 1
        routed = prod and ei >= 2
        if routed:
            oslot = ei % 2
            k.wait(DVE, s_dv, v_combT)
            bg = next_banks()
            k.wait(PE, s_dv, v_oh_of[ei])
            v_cb = None
            for q in range(NQ):
                v_cb = k.op(PE, lambda e, b=bg[q], q=q, oslot=oslot: e.matmul(
                    pbs[b][:, :],
                    onehot[:, oslot * 128:(oslot + 1) * 128],
                    combT[:, q * 512:(q + 1) * 512],
                    start=True, stop=True), s_pe, 1 if q == NQ - 1 else None)
            k.wait(DVE, s_pe, v_cb)
            for q in range(NQ):
                v_cbc = k.op(DVE, lambda e, b=bg[q], q=q: e.tensor_copy(
                    out=comb_bc[:, q * 512:(q + 1) * 512], in_=pbs[b][:, :]), s_dv, 1)
                release_bank(bg[q], s_dv, v_cbc)
            if ei + 1 < NEXP:
                emit_onehot(ei + 1)
        Hcur = hsbufs[ei % 2] if prod else None
        Hprev = hsbufs[(ei - 1) % 2] if cons else None

        def emit_ed_pair(ic):
            v_d0, v_d1 = staged[(ei, ic, "ed0")], staged[(ei, ic, "ed1")]
            groups = []
            for j, v_dc in ((0, v_d0), (1, v_d1)):
                hc = 2 * ic + j
                dslot = (ic % 2) * 2 + j
                k.wait(PE, v_dc[0], v_dc[1])
                k.wait(PE, s_dv, v_hs_of[ei - 1])
                bgD = next_banks()
                v_r = None
                for ic2 in range(4):
                    for q in range(NQ):
                        last = (ic2 == 3 and q == NQ - 1)
                        v_r = k.op(PE, lambda e, b=bgD[q], q=q, ic2=ic2, dslot=dslot, Hp=Hprev,
                                   s=(ic2 == 0), p=(ic2 == 3): e.matmul(
                                       pbs[b][:, :],
                                       ed_sb[:, dslot * 512 + ic2 * 128: dslot * 512 + ic2 * 128 + 128],
                                       Hp[:, ic2 * T + q * 512: ic2 * T + (q + 1) * 512],
                                       start=s, stop=p), s_pe, 1 if last else None)
                edslot_free[dslot] = (s_pe, v_r)
                groups.append((hc, bgD, v_r))
            return groups

        def emit_ed_evacs(groups):
            for hc, bgD, v_r in groups:
                k.wait(DVE, s_pe, v_r)
                for q in range(NQ):
                    v_acc = k.op(DVE, lambda e, b=bgD[q], q=q, hc=hc: e.tensor_tensor(
                        out=x_res[:, hc * T + q * 512: hc * T + (q + 1) * 512],
                        in0=x_res[:, hc * T + q * 512: hc * T + (q + 1) * 512],
                        in1=pbs[b][:, :], op=ALU.add), s_dv, 1)
                    release_bank(bgD[q], s_dv, v_acc)
                if ei - 1 == NEXP - 1:
                    v_out_final[hc] = v_acc

        for ic in range(4):
            if prod:
                cs_gu, v_ec = staged[(ei, ic, "eg")]
                _, v_uc = staged[(ei, ic, "eu")]
                eslot = ic % 2
                k.wait(PE, cs_gu, v_ec)
                k.wait(PE, s_dv, v_x2n_all)
                bgG = next_banks()
                v_g = None
                for kc in range(NCH):
                    for q in range(NQ):
                        last = (kc == NCH - 1 and q == NQ - 1)
                        v_g = k.op(PE, lambda e, b=bgG[q], q=q, kc=kc, eslot=eslot,
                                   s=(kc == 0), p=(kc == NCH - 1): e.matmul(
                                       pbs[b][:, :],
                                       eg_sb[:, eslot * 1024 + kc * 128: eslot * 1024 + kc * 128 + 128],
                                       X2n[:, kc * T + q * 512: kc * T + (q + 1) * 512],
                                       start=s, stop=p), s_pe, 1 if last else None)
                egslot_free[eslot] = (s_pe, v_g)
                k.wait(PE, cs_gu, v_uc)
                bgU = next_banks()
                v_u = None
                for kc in range(NCH):
                    for q in range(NQ):
                        last = (kc == NCH - 1 and q == NQ - 1)
                        v_u = k.op(PE, lambda e, b=bgU[q], q=q, kc=kc, eslot=eslot,
                                   s=(kc == 0), p=(kc == NCH - 1): e.matmul(
                                       pbs[b][:, :],
                                       eu_sb[:, eslot * 1024 + kc * 128: eslot * 1024 + kc * 128 + 128],
                                       X2n[:, kc * T + q * 512: kc * T + (q + 1) * 512],
                                       start=s, stop=p), s_pe, 1 if last else None)
                euslot_free[eslot] = (s_pe, v_u)
            if prod:
                for q in range(NQ):
                    sl = q % 2
                    if sguc_free[q] is not None:
                        k.wait(ACT, s_dv, sguc_free[q][1])
                    k.wait(ACT, s_pe, v_g)
                    v_sg = k.op(ACT, lambda e, b=bgG[q], q=q: e.activation(
                        out=sG[:, q * 512:(q + 1) * 512], in_=pbs[b][:, :], func=AF.Silu),
                        s_ac, 1)
                    release_bank(bgG[q], s_ac, v_sg)
                    k.wait(DVE, s_pe, v_u)
                    if routed:
                        k.wait(DVE, s_dv, v_cbc)
                        k.op(DVE, lambda e, b=bgU[q], q=q, sl=sl: e.tensor_tensor(
                            out=Uc[:, sl * 512:(sl + 1) * 512], in0=pbs[b][:, :],
                            in1=comb_bc[:, q * 512:(q + 1) * 512], op=ALU.mult))
                    else:
                        k.op(DVE, lambda e, b=bgU[q], sl=sl: e.tensor_copy(
                            out=Uc[:, sl * 512:(sl + 1) * 512], in_=pbs[b][:, :]))
                    k.wait(DVE, s_ac, v_sg)
                    v_hs = k.op(DVE, lambda e, q=q, sl=sl, ic=ic, Hc=Hcur: e.tensor_tensor(
                        out=Hc[:, ic * T + q * 512: ic * T + (q + 1) * 512],
                        in0=sG[:, q * 512:(q + 1) * 512], in1=Uc[:, sl * 512:(sl + 1) * 512],
                        op=ALU.mult), s_dv, 1)
                    sguc_free[q] = (s_dv, v_hs)
                    release_bank(bgU[q], s_dv, v_hs)
            stage_subiter(ei, ic + 1) if ic + 1 < 4 else stage_subiter(ei + 1, 0)
            if cons and PIPE_IL:
                emit_ed_evacs(emit_ed_pair(ic))
        if cons and not PIPE_IL:
            for ic in range(4):
                emit_ed_evacs(emit_ed_pair(ic))
        if prod:
            v_hs_of[ei] = v_hs
        if stage == "ffn_dbg" and ei == 2:
            k.op(DVE, lambda e: e.tensor_copy(out=dbg[:, 3 * 512:4 * 512], in_=comb_bc[:, 0:512]))
            k.op(DVE, lambda e: e.tensor_copy(out=dbg[:, 4 * 512:5 * 512], in_=Hs[:, 0:512]))

    if stage == "ffn_dbg":
        v_dbg = k.op(DVE, lambda e: e.engine_nop(), s_dv, 1)
        k.wait(SYNC, s_dv, v_dbg)
        k.op(SYNC, lambda e: e.dma_start(out=dbg_h[:, :], in_=dbg[:, :]), s_st, 16)
    v_st = None
    for c in range(NCH):
        k.wait(SYNC, s_dv, v_out_final[c])
        v_st = k.op(SYNC, lambda e, c=c: e.dma_start(
            out=out_h[c * 128:(c + 1) * 128, :], in_=x_res[:, c * T:(c + 1) * T]), s_st, 16)
    k.wait(SYNC, s_st, v_st)

    _play(nc, k)
    return nc


def _play(nc, k):
    with nc.Block() as block:
        def mk(name):
            def body(e):
                for fn in k.ops[name]:
                    fn(e)
            return body
        block.sync(mk(SYNC))
        block.tensor(mk(PE))
        block.vector(mk(DVE))
        block.scalar(mk(ACT))
        block.gpsimd(mk(POOL))


_NC_CACHE = {}


def _get_nc(stage="full"):
    if stage not in _NC_CACHE:
        _NC_CACHE[stage] = build_nc(stage)
    return _NC_CACHE[stage]


def _prep_inputs(inputs):
    f32 = np.float32
    x = np.asarray(inputs["L0_attn_x"], f32)
    wkT = np.ascontiguousarray(np.asarray(inputs["L0_attn_Wk"], f32).T)
    wvT = np.ascontiguousarray(np.asarray(inputs["L0_attn_Wv"], f32).T)
    wrT = np.ascontiguousarray(np.asarray(inputs["L0_attn_Wr"], f32).T)
    woT = np.ascontiguousarray(np.asarray(inputs["L0_attn_Wo"], f32).T)
    gateT = np.ascontiguousarray(np.asarray(inputs["L0_ffn_gate_w"], f32).T)
    gbias = np.ascontiguousarray(np.asarray(inputs["L0_ffn_gate_bias"], f32).reshape(1, 64))
    vecs = np.ascontiguousarray(np.concatenate(
        [np.asarray(inputs[k], f32).reshape(8, 128).T for k in
         ["L0_attn_time_decay", "L0_attn_time_first", "L0_attn_time_mix_k",
          "L0_attn_time_mix_v", "L0_attn_time_mix_r"]], axis=1))

    eg = np.asarray(inputs["L0_ffn_eg_w"], f32)
    eu = np.asarray(inputs["L0_ffn_eu_w"], f32)
    ed = np.asarray(inputs["L0_ffn_ed_w"], f32)
    sgT = np.asarray(inputs["L0_ffn_sg_w"], f32).T
    suT = np.asarray(inputs["L0_ffn_su_w"], f32).T
    sdT = np.asarray(inputs["L0_ffn_sd_w"], f32).T

    ew = np.empty((NEXP, EXP_SZ), f32)
    for h in range(2):
        ew[h, 0:EG_SZ] = np.ascontiguousarray(sgT[:, h * 512:(h + 1) * 512]).reshape(-1)
        ew[h, EG_SZ:2 * EG_SZ] = np.ascontiguousarray(suT[:, h * 512:(h + 1) * 512]).reshape(-1)
        ew[h, 2 * EG_SZ:] = np.ascontiguousarray(sdT[h * 512:(h + 1) * 512, :]).reshape(-1)
    egT = np.transpose(eg, (0, 2, 1))
    euT = np.transpose(eu, (0, 2, 1))
    edT = np.transpose(ed, (0, 2, 1))
    for e in range(NE):
        ew[2 + e, 0:EG_SZ] = np.ascontiguousarray(egT[e]).reshape(-1)
        ew[2 + e, EG_SZ:2 * EG_SZ] = np.ascontiguousarray(euT[e]).reshape(-1)
        ew[2 + e, 2 * EG_SZ:] = np.ascontiguousarray(edT[e]).reshape(-1)
    ew = np.ascontiguousarray(ew.reshape(-1))

    shared = dict(wkT=wkT, wvT=wvT, wrT=wrT, woT=woT, gateT=gateT, gbias=gbias,
                  vecs=vecs, EW=ew)
    in_maps = []
    for b in range(8):
        m = dict(shared)
        m["xT"] = np.ascontiguousarray(x[b].T)
        in_maps.append(m)
    return in_maps


def kernel(**inputs):
    in_maps = _prep_inputs(inputs)
    nc = _get_nc("full")
    res = run_bass_kernel_spmd(nc, in_maps, core_ids=list(range(8)))
    return np.stack([np.ascontiguousarray(r["outT"].T) for r in res.results], axis=0)


# version 13 (final)
# speedup vs baseline: 1.2293x; 1.2293x over previous
import numpy as np
import concourse.bass as bass
import concourse.mybir as mybir
from concourse.bass_utils import run_bass_kernel_spmd

F32 = mybir.dt.float32
F32R = mybir.dt.float32r
BF16 = mybir.dt.bfloat16
AF = mybir.ActivationFunctionType
ALU = mybir.AluOpType
AX = mybir.AxisListType

D = 1024
T = 2048
NCH = 8
TC = 512
NT = T // TC
NQ = T // 512
NE = 64
NEXP = NE + 2
EPS_LN = 1e-5
EG_SZ = D * 512
EXP_SZ = 3 * EG_SZ

SYNC, PE, DVE, ACT, POOL = "sync", "tensor", "vector", "scalar", "gpsimd"
CAST_GU_ENG = ACT
HS2_ALIAS = True
PIPE_IL = True
ENGS = [SYNC, PE, DVE, ACT, POOL]


class Sem:
    def __init__(self, name):
        self.name = name
        self.h = None
        self.n = 0


class Ctx:
    def __init__(self):
        self.ops = {e: [] for e in ENGS}
        self.last_wait = {e: {} for e in ENGS}

    def op(self, eng, fn, inc=None, amt=1):
        if inc is not None and amt:
            inc.n += amt
            self.ops[eng].append(lambda e, f=fn, s=inc, a=amt: f(e).then_inc(s.h, a))
            return inc.n
        self.ops[eng].append(fn)
        return None

    def wait(self, eng, sem, val):
        if val is None or val <= 0:
            return
        lw = self.last_wait[eng]
        if lw.get(sem.name, -1) >= val:
            return
        lw[sem.name] = val
        self.ops[eng].append(lambda e, s=sem, v=val: e.wait_ge(s.h, v))


def build_nc(stage="full"):
    nc = bass.Bass("TRN2", target_bir_lowering=False)

    xT_h = nc.dram_tensor("xT", [D, T], F32, kind="ExternalInput")
    wkT_h = nc.dram_tensor("wkT", [D, D], F32, kind="ExternalInput")
    wvT_h = nc.dram_tensor("wvT", [D, D], F32, kind="ExternalInput")
    wrT_h = nc.dram_tensor("wrT", [D, D], F32, kind="ExternalInput")
    woT_h = nc.dram_tensor("woT", [D, D], F32, kind="ExternalInput")
    gateT_h = nc.dram_tensor("gateT", [D, 64], F32, kind="ExternalInput")
    gbias_h = nc.dram_tensor("gbias", [1, 64], F32, kind="ExternalInput")
    vecs_h = nc.dram_tensor("vecs", [128, 40], F32, kind="ExternalInput")
    ew_h = nc.dram_tensor("EW", [NEXP * EXP_SZ], F32, kind="ExternalInput")
    out_h = nc.dram_tensor("outT", [D, T], F32, kind="ExternalOutput")
    dbg_h = nc.dram_tensor("dbg", [128, 8 * 512], F32, kind="ExternalOutput") if stage in ("attn_dbg", "ffn_dbg") else None

    BASE = 16384 + 4096
    LIMIT = 229376
    cur = [BASE]

    sb_offs = {}

    def sb(name, shape, dtype, at=None):
        nbytes = int(np.prod(shape[1:])) * mybir.dt.size(dtype)
        if at is None:
            at = cur[0]
            cur[0] = at + ((nbytes + 31) // 32) * 32
        assert at + nbytes <= LIMIT, f"SBUF overflow at {name}: {at}+{nbytes}"
        sb_offs[name] = at
        return nc.alloc_sbuf_tensor_at(name, shape, dtype, offset=at)

    x_res = sb("x_res", [128, NCH * T], F32)
    vecs = sb("vecs", [128, 40], F32)
    dv = sb("dv", [128, 48], F32)
    lam_tiles = sb("lam_tiles", [128, NCH * TC], F32)
    ones_tile = sb("ones_tile", [128, TC], F32)
    ones_col = sb("ones_col", [1, 128], F32)
    ones_red = sb("ones_red", [128, 1], F32)
    iota128 = sb("iota128", [128, 128], F32)
    pvec128 = sb("pvec128", [128, 1], F32)
    identity = sb("identity", [128, 128], F32)
    ones64 = sb("ones64", [64, 128], F32)
    pvec64 = sb("pvec64", [64, 1], F32)
    gbias_sb = sb("gbias_sb", [1, 64], F32)
    bias_bc = sb("bias_bc", [128, 64], F32)
    rows = sb("rows", [1, 512 * 4], F32)
    a_bc = sb("a_bc", [128, 512], F32)
    b_bc = sb("b_bc", [128, 512], F32)
    wstg = sb("wstg", [128, 2 * 1024], F32)
    onehot = sb("onehot", [64, 2 * 128], F32R)
    peq = sb("peq", [64, 2], F32)
    epsb = sb("epsb", [128, 1], F32)

    pa_base = cur[0]
    awb = sb("awb", [128, 4 * 1024], BF16)
    xk_all = sb("xk_all", [128, NCH * TC], BF16)
    xv_all = sb("xv_all", [128, NCH * TC], BF16)
    xr_all = sb("xr_all", [128, NCH * TC], BF16)
    rwkv_all = sb("rwkv_all", [128, NCH * TC], BF16)
    xn_prev = sb("xn_prev", [128, NCH], F32)
    bcarry = sb("bcarry", [128, NCH], F32)
    ccarry = sb("ccarry", [128, NCH], F32)
    bbuf = sb("bbuf", [128, 2 * 520], F32)
    cbuf = sb("cbuf", [128, 2 * 520], F32)
    scr = sb("scr", [128, 14 * 512], F32)
    xn_ext = sb("xn_ext", [128, 2 * 520], F32)
    a_bc4 = sb("a_bc4", [128, T], F32)
    b_bc4 = sb("b_bc4", [128, T], F32)
    pa_end = cur[0]

    cur[0] = pa_base
    X2n = sb("X2n", [128, NCH * T], BF16)
    comb = sb("comb", [128, 16 * 64], F32)
    combT = sb("combT", [64, T], F32R)
    comb_bc = sb("comb_bc", [128, T], F32)
    eg_sb = sb("eg_sb", [128, 2 * 1024], BF16)
    eu_sb = sb("eu_sb", [128, 2 * 1024], BF16)
    ed_sb = sb("ed_sb", [128, 4 * 512], BF16)
    gateT_sb = sb("gateT_sb", [128, NCH * 64], BF16)
    Hs = sb("Hs", [128, 4 * T], BF16)
    Hs2 = sb("Hs2", [128, 4 * T], BF16, at=sb_offs["lam_tiles"]) if HS2_ALIAS else Hs
    sG = sb("sG", [128, 4 * 512], F32)
    Uc = sb("Uc", [128, 2 * 512], F32)
    r_sig = sb("r_sig", [128, 64], F32)
    r_sc = sb("r_sc", [128, 64], F32)
    r_gs = sb("r_gs", [128, 8], F32)
    r_m8g = sb("r_m8g", [128, 8], F32)
    r_gm = sb("r_gm", [128, 8], F32)
    r_m64 = sb("r_m64", [128, 64], F32)
    r_m8 = sb("r_m8", [128, 8], F32)
    r_z = sb("r_z", [128, 4], F32)
    r_sel = sb("r_sel", [128, 64], F32)
    dbg = None
    if stage == "ffn_dbg":
        dbg = sb("dbg", [128, 8 * 512], F32, at=sb_offs["lam_tiles"])
    pbc_end = cur[0]
    import os
    if os.environ.get("PRINT_MAP"):
        print(f"MAP: pa_base={pa_base} pa_end={pa_end} pbc_end={pbc_end} LIMIT={LIMIT} slack_pa={LIMIT-pa_end} slack_bc={LIMIT-pbc_end}")
    assert max(pa_end, pbc_end) <= LIMIT

    pbs = [nc.psum_tensor(f"pb{i}", [128, 512], F32).__enter__() for i in range(8)]

    sems = {}
    for name in ["s_ld0", "s_lx", "s_w0", "s_w1", "s_pe", "s_dv", "s_ac", "s_po", "s_st"]:
        s = Sem(name)
        s.h = nc.semaphore(name).__enter__()
        sems[name] = s
    s_ld0, s_lx, s_pe, s_dv, s_ac, s_po, s_st = (sems[n] for n in
                                                 ["s_ld0", "s_lx", "s_pe", "s_dv", "s_ac", "s_po", "s_st"])
    s_w = [sems["s_w0"], sems["s_w1"]]

    k = Ctx()
    bank_free = {i: None for i in range(8)}

    def use_banks(ids):
        for b in ids:
            if bank_free[b] is not None:
                k.wait(PE, *bank_free[b])

    def release_bank(b, sem, val):
        bank_free[b] = (sem, val)

    def scr_ap(slot, w=512):
        return scr[:, slot * 512: slot * 512 + w]

    k.op(SYNC, lambda e: e.dma_start(out=vecs[:, :], in_=vecs_h[:, :]), s_ld0, 16)
    v_gb = k.op(SYNC, lambda e: e.dma_start(out=gbias_sb[:, :], in_=gbias_h[:, :]), s_ld0, 16)
    v_vecs = v_gb
    v_x = None
    for c in range(NCH):
        v_x = k.op(SYNC, lambda e, c=c: e.dma_start(
            out=x_res[:, c * T:(c + 1) * T], in_=xT_h[c * 128:(c + 1) * 128, :]), s_lx, 16)

    k.op(POOL, lambda e: e.iota(iota128[:, :], [[1, 128]], channel_multiplier=0,
                                allow_small_or_imprecise_dtypes=True))
    k.op(POOL, lambda e: e.iota(pvec128[:, :], [[1, 1]], channel_multiplier=1,
                                allow_small_or_imprecise_dtypes=True))
    v_s = k.op(POOL, lambda e: e.iota(pvec64[:, :], [[1, 1]], channel_multiplier=1,
                                      allow_small_or_imprecise_dtypes=True), s_po, 1)
    k.wait(POOL, s_po, v_s)
    k.op(POOL, lambda e: e.tensor_scalar(out=identity[:, :], in0=iota128[:, :],
                                         scalar1=pvec128[:, :1], scalar2=None, op0=ALU.is_equal))
    k.op(POOL, lambda e: e.memset(ones64[:, :], 1.0))
    k.op(POOL, lambda e: e.memset(ones_tile[:, :], 1.0))
    k.op(POOL, lambda e: e.memset(ones_col[:, :], 1.0))
    k.op(POOL, lambda e: e.memset(ones_red[:, :], 1.0))
    k.op(POOL, lambda e: e.memset(epsb[:, :], EPS_LN))
    k.op(POOL, lambda e: e.memset(xn_prev[:, :], 0.0))
    k.op(POOL, lambda e: e.memset(bcarry[:, :], 0.0))
    v_consts = k.op(POOL, lambda e: e.memset(ccarry[:, :], 0.0), s_po, 1)

    k.wait(ACT, s_ld0, v_vecs)
    k.op(ACT, lambda e: e.activation(out=dv[:, 0:8], in_=vecs[:, 0:8], func=AF.Exp))
    k.op(ACT, lambda e: e.activation(out=dv[:, 8:16], in_=dv[:, 0:8], func=AF.Exp, scale=-1.0))
    k.op(ACT, lambda e: e.activation(out=dv[:, 16:24], in_=vecs[:, 8:16], func=AF.Exp))
    k.wait(ACT, s_po, v_consts)
    v_lamt = None
    for c in range(NCH):
        v_lamt = k.op(ACT, lambda e, c=c: e.activation(
            out=lam_tiles[:, c * TC:(c + 1) * TC], in_=ones_tile[:, :],
            func=AF.Copy, scale=dv[:, 8 + c:9 + c]), s_ac, 1)
    k.wait(DVE, s_ld0, v_vecs)
    k.op(DVE, lambda e: e.tensor_scalar(out=dv[:, 24:32], in0=vecs[:, 16:24], scalar1=-1.0,
                                        scalar2=1.0, op0=ALU.mult, op1=ALU.add))
    k.op(DVE, lambda e: e.tensor_scalar(out=dv[:, 32:40], in0=vecs[:, 24:32], scalar1=-1.0,
                                        scalar2=1.0, op0=ALU.mult, op1=ALU.add))
    v_mix1 = k.op(DVE, lambda e: e.tensor_scalar(out=dv[:, 40:48], in0=vecs[:, 32:40],
                                                 scalar1=-1.0, scalar2=1.0, op0=ALU.mult,
                                                 op1=ALU.add), s_dv, 1)

    ln_state = {"v_last": None}

    def ln_stats(xsl, sqbufs, bsum, bss, ba, bb_, out_a=None, out_b=None):
        use_banks([bsum, bss])
        if ln_state["v_last"] is not None:
            k.wait(ACT, s_pe, ln_state["v_last"])
        v_prev = {}
        for c in range(NCH):
            if (c - 2) in v_prev:
                k.wait(ACT, s_pe, v_prev[c - 2])
            v_sq = k.op(ACT, lambda e, b=sqbufs[c % 2], xi=xsl(c): e.activation(
                out=b, in_=xi, func=AF.Square), s_ac, 1)
            k.op(PE, lambda e, c=c, xi=xsl(c): e.matmul(pbs[bsum][0:1, :], ones_red[:, :], xi,
                                             start=(c == 0), stop=(c == NCH - 1)))
            k.wait(PE, s_ac, v_sq)
            v_prev[c] = k.op(PE, lambda e, c=c, b=sqbufs[c % 2]: e.matmul(
                pbs[bss][0:1, :], ones_red[:, :], b,
                start=(c == 0), stop=(c == NCH - 1)), s_pe, 1)
        v_stats = v_prev[NCH - 1]
        ln_state["v_last"] = v_stats
        R_m = rows[0:1, 0:512]
        R_t1 = rows[0:1, 512:1024]
        R_t2 = rows[0:1, 1024:1536]
        R_a = rows[0:1, 512:1024]
        R_b = rows[0:1, 1536:2048]
        k.wait(DVE, s_pe, v_stats)
        k.op(DVE, lambda e: e.tensor_scalar(out=R_m, in0=pbs[bsum][0:1, :], scalar1=1.0 / D,
                                            scalar2=None, op0=ALU.mult))
        k.op(DVE, lambda e: e.tensor_scalar(out=R_t1, in0=pbs[bss][0:1, :], scalar1=1.0 / D,
                                            scalar2=None, op0=ALU.mult))
        k.op(DVE, lambda e: e.tensor_tensor(out=R_t2, in0=R_m, in1=R_m, op=ALU.mult))
        v_var = k.op(DVE, lambda e: e.tensor_tensor(out=R_t1, in0=R_t1, in1=R_t2,
                                                    op=ALU.subtract), s_dv, 1)
        k.wait(ACT, s_dv, v_var)
        v_sqr = k.op(ACT, lambda e: e.activation(out=R_t2, in_=R_t1, func=AF.Sqrt,
                                                 bias=epsb[0:1, :1]), s_ac, 1)
        k.wait(DVE, s_ac, v_sqr)
        v_s = k.op(DVE, lambda e: e.reciprocal(out=R_a, in_=R_t2), s_dv, 1)
        k.wait(DVE, s_dv, v_s)
        v_rows = k.op(DVE, lambda e: e.scalar_tensor_tensor(
            out=R_b, in0=R_m, scalar=-1.0, in1=R_a, op0=ALU.mult, op1=ALU.mult), s_dv, 1)
        use_banks([ba, bb_])
        k.wait(PE, s_dv, v_rows)
        k.op(PE, lambda e: e.matmul(pbs[ba][:, :], ones_col[:, :], R_a, start=True, stop=True))
        v_bcm = k.op(PE, lambda e: e.matmul(pbs[bb_][:, :], ones_col[:, :], R_b,
                                            start=True, stop=True), s_pe, 1)
        if out_a is None:
            out_a, out_b = a_bc[:, :], b_bc[:, :]
        k.wait(DVE, s_pe, v_bcm)
        k.op(DVE, lambda e, oa=out_a: e.tensor_copy(out=oa, in_=pbs[ba][:, :]))
        v_ab = k.op(DVE, lambda e, ob=out_b: e.tensor_copy(out=ob, in_=pbs[bb_][:, :]), s_dv, 1)
        for b in (bsum, bss, ba, bb_):
            release_bank(b, s_dv, v_ab)
        return v_ab

    wslot_free = [None, None]
    stg_idx = [0]

    def stage_slab(src_ap, dstbuf, dslot, width, slot_free, cast_eng=POOL):
        cast_sem = {POOL: s_po, ACT: s_ac, DVE: s_dv}[cast_eng]
        sslot = stg_idx[0] % 2
        stg_idx[0] += 1
        if wslot_free[sslot] is not None:
            k.wait(SYNC, *wslot_free[sslot])
        dst = wstg[:, sslot * 1024: sslot * 1024 + width]
        dst3 = dst.rearrange("p (a b) -> p a b", b=128)
        v_d = k.op(SYNC, lambda e, s=src_ap, o=dst3: e.dma_start(out=o, in_=s), s_w[sslot], 16)
        if slot_free[dslot] is not None:
            k.wait(cast_eng, *slot_free[dslot])
        k.wait(cast_eng, s_w[sslot], v_d)
        if cast_eng == ACT:
            v_c = k.op(ACT, lambda e, ss=sslot, dstbuf=dstbuf, dslot=dslot, width=width: e.copy(
                out=dstbuf[:, dslot * width:(dslot + 1) * width],
                in_=wstg[:, ss * 1024: ss * 1024 + width]), s_ac, 1)
        else:
            v_c = k.op(cast_eng, lambda e, ss=sslot, dstbuf=dstbuf, dslot=dslot, width=width: e.tensor_copy(
                out=dstbuf[:, dslot * width:(dslot + 1) * width],
                in_=wstg[:, ss * 1024: ss * 1024 + width]), cast_sem, 1)
        wslot_free[sslot] = (cast_sem, v_c)
        return (cast_sem, v_c)

    wdrams = {"k": wkT_h, "v": wvT_h, "r": wrT_h, "o": woT_h}
    awslot_free = [None, None, None, None]
    aw_staged = {}

    def stage_attn_slab(j):
        aslot = j % 4
        if j < 24:
            proj, ec = ("k", "v", "r")[j % 3], j // 3
        else:
            proj, ec = "o", j - 24
        src = bass.AP(wdrams[proj], ec * 128, [[D, 128], [128 * D, 8], [1, 128]])
        sem, v_c = stage_slab(src, awb, aslot, 1024, awslot_free, cast_eng=ACT)
        aw_staged[j] = (aslot, (sem, v_c))

    MIX = [("k", xk_all, 16, 24), ("v", xv_all, 24, 32), ("r", xr_all, 32, 40)]
    k.wait(ACT, s_lx, v_x)
    k.wait(PE, s_lx, v_x)
    k.wait(PE, s_po, v_consts)

    def xsl_of(tile):
        t0 = tile * TC

        def xsl(c, t0=t0):
            return x_res[:, c * T + t0: c * T + t0 + TC]
        return xsl

    for tile in range(NT):
        ln_stats(xsl_of(tile), [scr_ap(0), scr_ap(1)], 0, 1, 2, 3,
                 out_a=a_bc4[:, tile * TC:(tile + 1) * TC],
                 out_b=b_bc4[:, tile * TC:(tile + 1) * TC])
    slot_user = {}
    v_x2_tile = None
    v_allmix_of = {}
    k.wait(DVE, s_ac, v_lamt)
    k.wait(DVE, s_dv, v_mix1)

    slot_dv = {}
    v_pchunk_of = {}
    pend = []

    def emit_wkv_tail(item):
        ec, sl, pt, num, rec, v_dvchunk = item
        k.wait(ACT, s_dv, v_dvchunk)
        k.op(ACT, lambda e, rec=rec, num=num: e.activation(
            out=num, in_=rec, func=AF.Ln))
        v_rec = k.op(ACT, lambda e, num=num, rec=rec: e.activation(
            out=rec, in_=num, func=AF.Exp, scale=-1.0), s_ac, 1)
        k.wait(POOL, s_ac, v_rec)
        v_pchunk_of[ec] = k.op(POOL, lambda e, pt=pt, rec=rec, ec=ec: e.tensor_tensor(
            out=rwkv_all[:, ec * TC:(ec + 1) * TC], in0=pt, in1=rec,
            op=ALU.mult), s_po, 1)
        slot_user[sl] = (slot_dv[sl], v_pchunk_of[ec])

    def emit_mix_chunk(tile, c):
        xsl = xsl_of(tile)
        abc = a_bc4[:, tile * TC:(tile + 1) * TC]
        bbc = b_bc4[:, tile * TC:(tile + 1) * TC]
        xe = xn_ext[:, (c % 2) * 520:(c % 2) * 520 + 513]
        k.op(DVE, lambda e, xe=xe, c=c: e.tensor_copy(out=xe[:, 0:1],
                                                      in_=xn_prev[:, c:c + 1]))
        k.op(DVE, lambda e, xe=xe, xi=xsl(c), abc=abc: e.tensor_tensor(
            out=xe[:, 1:513], in0=xi, in1=abc, op=ALU.mult))
        v_s = k.op(DVE, lambda e, xe=xe, bbc=bbc: e.tensor_tensor(
            out=xe[:, 1:513], in0=xe[:, 1:513], in1=bbc, op=ALU.add), s_dv, 1)
        k.wait(DVE, s_dv, v_s)
        k.op(DVE, lambda e, xe=xe, c=c: e.tensor_copy(out=xn_prev[:, c:c + 1],
                                                      in_=xe[:, 512:513]))
        v_mx = None
        for pname, xall, mco, m1co in MIX:
            tmp = scr_ap(c % 2)
            k.op(DVE, lambda e, tmp=tmp, xe=xe, m1co=m1co, c=c: e.tensor_scalar(
                out=tmp, in0=xe[:, 0:512], scalar1=dv[:, m1co + c:m1co + c + 1],
                scalar2=None, op0=ALU.mult))
            v_mx = k.op(DVE, lambda e, xall=xall, tmp=tmp, xe=xe, mco=mco, c=c:
                        e.scalar_tensor_tensor(
                            out=xall[:, c * TC:(c + 1) * TC], in0=xe[:, 1:513],
                            scalar=vecs[:, mco + c:mco + c + 1], in1=tmp,
                            op0=ALU.mult, op1=ALU.add), s_dv, 1)
        return v_mx

    for c in range(NCH):
        v_allmix_of[0] = emit_mix_chunk(0, c)

    for tile in range(NT):
        xsl = xsl_of(tile)
        v_allmix = v_allmix_of[tile]

        pbank = [0]
        for j in range(3):
            stage_attn_slab(j)
        for ec in range(NCH):
            grp = {}
            for pi, (proj, xall) in enumerate((("k", xk_all), ("v", xv_all), ("r", xr_all))):
                b = pbank[0]
                pbank[0] = (pbank[0] + 1) % 8
                use_banks([b])
                aslot, (csem, v_c) = aw_staged[ec * 3 + pi]
                k.wait(PE, csem, v_c)
                k.wait(PE, s_dv, v_allmix)
                v_mm = None
                for kc in range(NCH):
                    v_mm = k.op(PE, lambda e, b=b, aslot=aslot, kc=kc, xall=xall,
                                s=(kc == 0), p=(kc == NCH - 1): e.matmul(
                                    pbs[b][:, :],
                                    awb[:, aslot * 1024 + kc * 128: aslot * 1024 + kc * 128 + 128],
                                    xall[:, kc * TC:(kc + 1) * TC], start=s, stop=p),
                                s_pe, 1 if kc == NCH - 1 else None)
                grp[proj] = (b, v_mm)
                awslot_free[aslot] = (s_pe, v_mm)
                if ec * 3 + pi + 3 < 32:
                    stage_attn_slab(ec * 3 + pi + 3)
            sl = ec % 2
            ek, vv, er = scr_ap(2 + sl), scr_ap(4 + sl), scr_ap(6 + sl)
            pt, num, rec = scr_ap(8 + sl), scr_ap(10 + sl), scr_ap(12 + sl)
            den = vv
            if sl in slot_user:
                dve_snap, pool_snap = slot_user[sl]
                k.wait(ACT, s_dv, dve_snap)
                k.wait(ACT, s_po, pool_snap)
            bK, vK = grp["k"]
            bV, vV = grp["v"]
            bR, vR = grp["r"]
            k.wait(ACT, s_pe, vK)
            v_ek = k.op(ACT, lambda e, bK=bK, ek=ek: e.activation(
                out=ek, in_=pbs[bK][:, :], func=AF.Exp), s_ac, 1)
            release_bank(bK, s_ac, v_ek)
            k.wait(ACT, s_pe, vR)
            v_er = k.op(ACT, lambda e, bR=bR, er=er: e.activation(
                out=er, in_=pbs[bR][:, :], func=AF.Exp), s_ac, 1)
            release_bank(bR, s_ac, v_er)
            k.wait(ACT, s_pe, vV)
            v_vv = k.op(ACT, lambda e, bV=bV, vv=vv: e.copy(out=vv, in_=pbs[bV][:, :]), s_ac, 1)
            release_bank(bV, s_ac, v_vv)
            if sl in slot_user:
                k.wait(POOL, s_dv, slot_user[sl][0])
            k.wait(POOL, s_ac, v_vv)
            v_p = k.op(POOL, lambda e, ek=ek, vv=vv, pt=pt: e.tensor_tensor(
                out=pt, in0=ek, in1=vv, op=ALU.mult), s_po, 1)
            bb = bbuf[:, sl * 520: sl * 520 + 513]
            cb = cbuf[:, sl * 520: sl * 520 + 513]
            lamt = lam_tiles[:, ec * TC:(ec + 1) * TC]
            k.wait(DVE, s_po, v_p)
            k.wait(DVE, s_ac, v_vv)
            v_s = k.op(DVE, lambda e, bb=bb, ec=ec: e.tensor_copy(
                out=bb[:, 0:1], in_=bcarry[:, ec:ec + 1]), s_dv, 1)
            v_s = k.op(DVE, lambda e, cb=cb, ec=ec: e.tensor_copy(
                out=cb[:, 0:1], in_=ccarry[:, ec:ec + 1]), s_dv, 1)
            k.wait(DVE, s_dv, v_s)
            k.op(DVE, lambda e, bb=bb, lamt=lamt, ek=ek: e.tensor_tensor_scan(
                out=bb[:, 1:513], data0=lamt, data1=ek, initial=bb[:, 0:1],
                op0=ALU.mult, op1=ALU.add))
            v_s = k.op(DVE, lambda e, cb=cb, lamt=lamt, pt=pt: e.tensor_tensor_scan(
                out=cb[:, 1:513], data0=lamt, data1=pt, initial=cb[:, 0:1],
                op0=ALU.mult, op1=ALU.add), s_dv, 1)
            k.wait(DVE, s_dv, v_s)
            k.op(DVE, lambda e, bb=bb, ec=ec: e.tensor_copy(out=bcarry[:, ec:ec + 1],
                                                            in_=bb[:, 512:513]))
            k.op(DVE, lambda e, cb=cb, ec=ec: e.tensor_copy(out=ccarry[:, ec:ec + 1],
                                                            in_=cb[:, 512:513]))
            k.op(DVE, lambda e, den=den, ek=ek, bb=bb, ec=ec: e.scalar_tensor_tensor(
                out=den, in0=ek, scalar=dv[:, 16 + ec:17 + ec], in1=bb[:, 0:TC],
                op0=ALU.mult, op1=ALU.add))
            k.op(DVE, lambda e, num=num, pt=pt, cb=cb, ec=ec: e.scalar_tensor_tensor(
                out=num, in0=pt, scalar=dv[:, 16 + ec:17 + ec], in1=cb[:, 0:TC],
                op0=ALU.mult, op1=ALU.add))
            k.op(DVE, lambda e, den=den, er=er, rec=rec: e.tensor_tensor(
                out=rec, in0=den, in1=er, op=ALU.mult))
            k.op(DVE, lambda e, den=den, rec=rec: e.tensor_tensor(
                out=rec, in0=rec, in1=den, op=ALU.add))
            v_dvchunk = k.op(DVE, lambda e, num=num, er=er, pt=pt: e.tensor_tensor(
                out=pt, in0=num, in1=er, op=ALU.mult), s_dv, 1)
            pend.append((ec, sl, pt, num, rec, v_dvchunk))
            slot_dv[sl] = v_dvchunk
            if len(pend) > 1:
                emit_wkv_tail(pend.pop(0))
        emit_wkv_tail(pend.pop(0))
        v_rw_all = v_pchunk_of[NCH - 1]

        for ec in range(NCH):
            aslot, (csem, v_c) = aw_staged[24 + ec]
            b = pbank[0]
            pbank[0] = (pbank[0] + 1) % 8
            use_banks([b])
            k.wait(PE, csem, v_c)
            k.wait(PE, s_po, v_rw_all)
            v_mm = None
            for kc in range(NCH):
                v_mm = k.op(PE, lambda e, b=b, aslot=aslot, kc=kc,
                            s=(kc == 0), p=(kc == NCH - 1): e.matmul(
                                pbs[b][:, :],
                                awb[:, aslot * 1024 + kc * 128: aslot * 1024 + kc * 128 + 128],
                                rwkv_all[:, kc * TC:(kc + 1) * TC], start=s, stop=p),
                            s_pe, 1 if kc == NCH - 1 else None)
            awslot_free[aslot] = (s_pe, v_mm)
            if 24 + ec + 3 < 32:
                stage_attn_slab(24 + ec + 3)
            if tile + 1 < NT:
                v_allmix_of[tile + 1] = emit_mix_chunk(tile + 1, ec)
            k.wait(DVE, s_pe, v_mm)
            v_x2_tile = k.op(DVE, lambda e, b=b, xi=xsl(ec): e.tensor_tensor(
                out=xi, in0=xi, in1=pbs[b][:, :], op=ALU.add), s_dv, 1)
            release_bank(b, s_dv, v_x2_tile)

    if stage == "attn":
        v_st = None
        for c in range(NCH):
            k.wait(SYNC, s_dv, v_x2_tile)
            v_st = k.op(SYNC, lambda e, c=c: e.dma_start(
                out=out_h[c * 128:(c + 1) * 128, :], in_=x_res[:, c * T:(c + 1) * T]), s_st, 16)
        k.wait(SYNC, s_st, v_st)
        _play(nc, k)
        return nc

    v_gate_cast = None
    for kc in range(NCH):
        sslot = stg_idx[0] % 2
        stg_idx[0] += 1
        if wslot_free[sslot] is not None:
            k.wait(SYNC, *wslot_free[sslot])
        v_d = k.op(SYNC, lambda e, kc=kc, ss=sslot: e.dma_start(
            out=wstg[:, ss * 1024: ss * 1024 + 64],
            in_=gateT_h[kc * 128:(kc + 1) * 128, :]), s_w[sslot], 16)
        k.wait(POOL, s_w[sslot], v_d)
        v_gate_cast = k.op(POOL, lambda e, kc=kc, ss=sslot: e.tensor_copy(
            out=gateT_sb[:, kc * 64:(kc + 1) * 64],
            in_=wstg[:, ss * 1024: ss * 1024 + 64]), s_po, 1)
        wslot_free[sslot] = (s_po, v_gate_cast)

    use_banks([0])
    k.wait(PE, s_ld0, v_gb)
    k.wait(PE, s_po, v_consts)
    v_bb = k.op(PE, lambda e: e.matmul(pbs[0][:, 0:64], ones_col[:, :], gbias_sb[:, :],
                                       start=True, stop=True), s_pe, 1)
    k.wait(DVE, s_pe, v_bb)
    v_bbc = k.op(DVE, lambda e: e.tensor_copy(out=bias_bc[:, :], in_=pbs[0][:, 0:64]), s_dv, 1)
    release_bank(0, s_dv, v_bbc)

    k.wait(ACT, s_dv, v_x2_tile)
    k.wait(PE, s_dv, v_x2_tile)
    v_x2n_all = None
    for q in range(NQ):
        t0 = q * 512

        def x2sl(c, t0=t0):
            return x_res[:, c * T + t0: c * T + t0 + 512]

        sqb = [sG[:, 0:512], sG[:, 512:1024]]
        v_ab2 = ln_stats(x2sl, sqb, 1, 2, 3, 4)
        for c in range(NCH):
            y = Uc[:, (c % 2) * 512:(c % 2) * 512 + 512]
            k.op(DVE, lambda e, y=y, xi=x2sl(c): e.tensor_tensor(out=y, in0=xi, in1=a_bc[:, :],
                                                                 op=ALU.mult))
            v_x2n_all = k.op(DVE, lambda e, c=c, y=y, t0=t0: e.tensor_tensor(
                out=X2n[:, c * T + t0: c * T + t0 + 512], in0=y, in1=b_bc[:, :],
                op=ALU.add), s_dv, 1)

    v_comb = None
    gb_bank = [5]
    for tc in range(16):
        b = gb_bank[0]
        gb_bank[0] = 5 + ((gb_bank[0] - 5 + 1) % 3)
        use_banks([b])
        k.wait(PE, s_po, v_gate_cast)
        k.wait(PE, s_dv, v_x2n_all)
        v_mm = None
        for kc in range(NCH):
            v_mm = k.op(PE, lambda e, b=b, kc=kc, tc=tc, s=(kc == 0), p=(kc == NCH - 1):
                        e.matmul(pbs[b][:, 0:64],
                                 X2n[:, kc * T + tc * 128: kc * T + tc * 128 + 128],
                                 gateT_sb[:, kc * 64:(kc + 1) * 64], start=s, stop=p),
                        s_pe, 1 if kc == NCH - 1 else None)
        if v_comb is not None:
            k.wait(ACT, s_dv, v_comb)
        k.wait(ACT, s_pe, v_mm)
        v_sig = k.op(ACT, lambda e, b=b: e.activation(out=r_sig[:, :], in_=pbs[b][:, 0:64],
                                                      func=AF.Sigmoid), s_ac, 1)
        release_bank(b, s_ac, v_sig)
        k.wait(DVE, s_ac, v_sig)
        k.wait(DVE, s_dv, v_bbc)
        k.op(DVE, lambda e: e.tensor_tensor(out=r_sc[:, :], in0=r_sig[:, :], in1=bias_bc[:, :],
                                            op=ALU.add))
        v_s = k.op(DVE, lambda e: e.tensor_reduce(
            out=r_gs[:, :].rearrange("p (g o) -> p g o", g=8),
            in_=r_sc[:, :].rearrange("p (g ee) -> p g ee", g=8), axis=AX.X, op=ALU.max), s_dv, 1)
        k.wait(DVE, s_dv, v_s)
        v_s = k.op(DVE, lambda e: e.max(out=r_m8g[:, :], in_=r_gs[:, :]), s_dv, 1)
        k.wait(DVE, s_dv, v_s)
        v_s = k.op(DVE, lambda e: e.tensor_scalar(out=r_gm[:, :], in0=r_gs[:, :],
                                                   scalar1=r_m8g[:, 3:4], scalar2=None,
                                                   op0=ALU.is_ge), s_dv, 1)
        k.wait(DVE, s_dv, v_s)
        for g in range(8):
            v_s = k.op(DVE, lambda e, g=g: e.tensor_scalar(
                out=r_m64[:, g * 8:(g + 1) * 8], in0=r_sc[:, g * 8:(g + 1) * 8],
                scalar1=r_gm[:, g:g + 1], scalar2=None, op0=ALU.mult), s_dv, 1)
        k.wait(DVE, s_dv, v_s)
        v_s = k.op(DVE, lambda e: e.max(out=r_m8[:, :], in_=r_m64[:, :]), s_dv, 1)
        k.wait(DVE, s_dv, v_s)
        v_s = k.op(DVE, lambda e: e.tensor_reduce(out=r_z[:, 0:1], in_=r_m8[:, 0:6], axis=AX.X,
                                                  op=ALU.add), s_dv, 1)
        k.wait(DVE, s_dv, v_s)
        v_s = k.op(DVE, lambda e: e.tensor_scalar(out=r_z[:, 1:2], in0=r_z[:, 0:1], scalar1=1e-8,
                                                   scalar2=None, op0=ALU.add), s_dv, 1)
        k.wait(DVE, s_dv, v_s)
        v_s = k.op(DVE, lambda e: e.reciprocal(out=r_z[:, 2:3], in_=r_z[:, 1:2]), s_dv, 1)
        k.wait(DVE, s_dv, v_s)
        v_s = k.op(DVE, lambda e: e.tensor_scalar(out=r_sel[:, :], in0=r_m64[:, :],
                                                   scalar1=r_m8[:, 5:6], scalar2=None,
                                                   op0=ALU.is_ge), s_dv, 1)
        k.wait(DVE, s_dv, v_s)
        v_comb = k.op(DVE, lambda e, tc=tc: e.scalar_tensor_tensor(
            out=comb[:, tc * 64:(tc + 1) * 64], in0=r_m64[:, :], scalar=r_z[:, 2:3],
            in1=r_sel[:, :], op0=ALU.mult, op1=ALU.mult), s_dv, 1)
        if stage == "ffn_dbg" and tc == 0:
            k.op(DVE, lambda e: e.tensor_copy(out=dbg[:, 5 * 512:5 * 512 + 64], in_=r_sc[:, :]))
            k.op(DVE, lambda e: e.tensor_copy(out=dbg[:, 6 * 512:6 * 512 + 64], in_=r_sel[:, :]))
            k.op(DVE, lambda e: e.tensor_copy(out=dbg[:, 7 * 512:7 * 512 + 8], in_=r_m8[:, :]))
            k.op(DVE, lambda e: e.tensor_copy(out=dbg[:, 7 * 512 + 8:7 * 512 + 12], in_=r_z[:, :]))
            k.op(DVE, lambda e: e.tensor_copy(out=dbg[:, 7 * 512 + 16:7 * 512 + 24], in_=r_gs[:, :]))
            k.op(DVE, lambda e: e.tensor_copy(out=dbg[:, 7 * 512 + 24:7 * 512 + 32], in_=r_m8g[:, :]))
            k.op(DVE, lambda e: e.tensor_copy(out=dbg[:, 7 * 512 + 32:7 * 512 + 40], in_=r_gm[:, :]))
            k.op(DVE, lambda e: e.tensor_copy(out=dbg[:, 6 * 512 + 64:6 * 512 + 128], in_=r_m64[:, :]))

    v_combT = None
    for tc in range(16):
        b = 5 + (tc % 3)
        use_banks([b])
        k.wait(PE, s_dv, v_comb)
        k.wait(PE, s_po, v_consts)
        v_tp = k.op(PE, lambda e, b=b, tc=tc: e.transpose(
            pbs[b][0:64, 0:128], comb[:, tc * 64:(tc + 1) * 64], identity[:, :]), s_pe, 1)
        k.wait(DVE, s_pe, v_tp)
        k.op(DVE, lambda e, b=b: e.tensor_copy(
            out=sG[0:64, 0:128], in_=pbs[b][0:64, 0:128]))
        v_combT = k.op(DVE, lambda e, tc=tc: e.tensor_copy(
            out=combT[:, tc * 128:(tc + 1) * 128], in_=sG[0:64, 0:128]), s_dv, 1)
        release_bank(b, s_dv, v_combT)

    if stage == "ffn_dbg":
        k.op(DVE, lambda e: e.tensor_copy(out=dbg[:, 0:512], in_=X2n[:, 0:512]))
        k.op(DVE, lambda e: e.tensor_copy(out=dbg[:, 512:1536], in_=comb[:, :]))

    if stage == "ln2":
        v_st = None
        for c in range(NCH):
            k.wait(SYNC, s_dv, v_combT)
            v_st = k.op(SYNC, lambda e, c=c: e.dma_start(
                out=out_h[c * 128:(c + 1) * 128, :], in_=x_res[:, c * T:(c + 1) * T]), s_st, 16)
        k.wait(SYNC, s_st, v_st)
        _play(nc, k)
        return nc

    egslot_free = [None, None]
    euslot_free = [None, None]
    edslot_free = [None, None, None, None]

    def exp_slab_src(e_idx, kind, idx):
        base = e_idx * EXP_SZ + {"eg": 0, "eu": EG_SZ, "ed": 2 * EG_SZ}[kind]
        if kind in ("eg", "eu"):
            return bass.AP(ew_h, base + idx * 128, [[512, 128], [128 * 512, 8], [1, 128]])
        return bass.AP(ew_h, base + idx * 128, [[1024, 128], [128 * 1024, 4], [1, 128]])

    sguc_free = [None, None, None, None]
    uc_free = [None, None]
    v_out_final = {}
    hsbufs = [Hs, Hs2]
    gpar = [0]

    def next_banks():
        b = [0, 1, 2, 3] if gpar[0] == 0 else [4, 5, 6, 7]
        gpar[0] ^= 1
        use_banks(b)
        return b

    v_hs_of = {}
    v_cbc = None
    v_hs = None
    v_oh_of = {}
    staged = {}

    def stage_subiter(ej, ic):
        if ej > NEXP or ic > 3:
            return
        if ej < NEXP:
            staged[(ej, ic, "eg")] = stage_slab(exp_slab_src(ej, "eg", ic), eg_sb,
                                                ic % 2, 1024, egslot_free,
                                                cast_eng=CAST_GU_ENG)
            staged[(ej, ic, "eu")] = stage_slab(exp_slab_src(ej, "eu", ic), eu_sb,
                                                ic % 2, 1024, euslot_free,
                                                cast_eng=CAST_GU_ENG)
        if ej >= 1:
            staged[(ej, ic, "ed0")] = stage_slab(exp_slab_src(ej - 1, "ed", 2 * ic),
                                                 ed_sb, (ic % 2) * 2, 512, edslot_free,
                                                 cast_eng=POOL)
            staged[(ej, ic, "ed1")] = stage_slab(exp_slab_src(ej - 1, "ed", 2 * ic + 1),
                                                 ed_sb, (ic % 2) * 2 + 1, 512, edslot_free,
                                                 cast_eng=POOL)

    stage_subiter(0, 0)

    def emit_onehot(ej):
        oslot = ej % 2
        v_s = k.op(DVE, lambda e, oslot=oslot, ee=float(ej - 2): e.tensor_scalar(
            out=peq[:, oslot:oslot + 1], in0=pvec64[:, :], scalar1=ee, scalar2=None,
            op0=ALU.is_equal), s_dv, 1)
        k.wait(DVE, s_dv, v_s)
        v_oh_of[ej] = k.op(DVE, lambda e, oslot=oslot: e.tensor_scalar(
            out=onehot[:, oslot * 128:(oslot + 1) * 128], in0=ones64[:, :],
            scalar1=peq[:, oslot:oslot + 1], scalar2=None, op0=ALU.mult), s_dv, 1)

    emit_onehot(2)

    for ei in range(NEXP + 1):
        prod = ei < NEXP
        cons = ei >= 1
        routed = prod and ei >= 2
        if routed:
            oslot = ei % 2
            k.wait(DVE, s_dv, v_combT)
            bg = next_banks()
            k.wait(PE, s_dv, v_oh_of[ei])
            v_cb = None
            for q in range(NQ):
                v_cb = k.op(PE, lambda e, b=bg[q], q=q, oslot=oslot: e.matmul(
                    pbs[b][:, :],
                    onehot[:, oslot * 128:(oslot + 1) * 128],
                    combT[:, q * 512:(q + 1) * 512],
                    start=True, stop=True), s_pe, 1 if q == NQ - 1 else None)
            k.wait(DVE, s_pe, v_cb)
            for q in range(NQ):
                v_cbc = k.op(DVE, lambda e, b=bg[q], q=q: e.tensor_copy(
                    out=comb_bc[:, q * 512:(q + 1) * 512], in_=pbs[b][:, :]), s_dv, 1)
                release_bank(bg[q], s_dv, v_cbc)
            if ei + 1 < NEXP:
                emit_onehot(ei + 1)
        Hcur = hsbufs[ei % 2] if prod else None
        Hprev = hsbufs[(ei - 1) % 2] if cons else None

        def emit_ed_pair(ic):
            v_d0, v_d1 = staged[(ei, ic, "ed0")], staged[(ei, ic, "ed1")]
            groups = []
            for j, v_dc in ((0, v_d0), (1, v_d1)):
                hc = 2 * ic + j
                dslot = (ic % 2) * 2 + j
                k.wait(PE, v_dc[0], v_dc[1])
                k.wait(PE, s_po, v_hs_of[ei - 1])
                bgD = next_banks()
                v_r = None
                for ic2 in range(4):
                    for q in range(NQ):
                        last = (ic2 == 3 and q == NQ - 1)
                        v_r = k.op(PE, lambda e, b=bgD[q], q=q, ic2=ic2, dslot=dslot, Hp=Hprev,
                                   s=(ic2 == 0), p=(ic2 == 3): e.matmul(
                                       pbs[b][:, :],
                                       ed_sb[:, dslot * 512 + ic2 * 128: dslot * 512 + ic2 * 128 + 128],
                                       Hp[:, ic2 * T + q * 512: ic2 * T + (q + 1) * 512],
                                       start=s, stop=p), s_pe, 1 if last else None)
                edslot_free[dslot] = (s_pe, v_r)
                groups.append((hc, bgD, v_r))
            return groups

        def emit_ed_evacs(groups):
            for hc, bgD, v_r in groups:
                k.wait(DVE, s_pe, v_r)
                for q in range(NQ):
                    v_acc = k.op(DVE, lambda e, b=bgD[q], q=q, hc=hc: e.tensor_tensor(
                        out=x_res[:, hc * T + q * 512: hc * T + (q + 1) * 512],
                        in0=x_res[:, hc * T + q * 512: hc * T + (q + 1) * 512],
                        in1=pbs[b][:, :], op=ALU.add), s_dv, 1)
                    release_bank(bgD[q], s_dv, v_acc)
                if ei - 1 == NEXP - 1:
                    v_out_final[hc] = v_acc

        for ic in range(4):
            stage_subiter(ei, ic + 1) if ic + 1 < 4 else stage_subiter(ei + 1, 0)
            if prod:
                cs_gu, v_ec = staged[(ei, ic, "eg")]
                _, v_uc = staged[(ei, ic, "eu")]
                eslot = ic % 2
                k.wait(PE, cs_gu, v_ec)
                k.wait(PE, s_dv, v_x2n_all)
                bgG = next_banks()
                v_g = None
                for kc in range(NCH):
                    for q in range(NQ):
                        last = (kc == NCH - 1 and q == NQ - 1)
                        v_g = k.op(PE, lambda e, b=bgG[q], q=q, kc=kc, eslot=eslot,
                                   s=(kc == 0), p=(kc == NCH - 1): e.matmul(
                                       pbs[b][:, :],
                                       eg_sb[:, eslot * 1024 + kc * 128: eslot * 1024 + kc * 128 + 128],
                                       X2n[:, kc * T + q * 512: kc * T + (q + 1) * 512],
                                       start=s, stop=p), s_pe, 1 if last else None)
                egslot_free[eslot] = (s_pe, v_g)
                k.wait(PE, cs_gu, v_uc)
                bgU = next_banks()
                v_u = None
                for kc in range(NCH):
                    for q in range(NQ):
                        last = (kc == NCH - 1 and q == NQ - 1)
                        v_u = k.op(PE, lambda e, b=bgU[q], q=q, kc=kc, eslot=eslot,
                                   s=(kc == 0), p=(kc == NCH - 1): e.matmul(
                                       pbs[b][:, :],
                                       eu_sb[:, eslot * 1024 + kc * 128: eslot * 1024 + kc * 128 + 128],
                                       X2n[:, kc * T + q * 512: kc * T + (q + 1) * 512],
                                       start=s, stop=p), s_pe, 1 if last else None)
                euslot_free[eslot] = (s_pe, v_u)
            if prod:
                for q in range(NQ):
                    sl = q % 2
                    if sguc_free[q] is not None:
                        k.wait(ACT, *sguc_free[q])
                    k.wait(ACT, s_pe, v_g)
                    v_sg = k.op(ACT, lambda e, b=bgG[q], q=q: e.activation(
                        out=sG[:, q * 512:(q + 1) * 512], in_=pbs[b][:, :], func=AF.Silu),
                        s_ac, 1)
                    release_bank(bgG[q], s_ac, v_sg)
                    k.wait(DVE, s_pe, v_u)
                    if uc_free[sl] is not None:
                        k.wait(DVE, s_po, uc_free[sl])
                    if routed:
                        k.wait(DVE, s_dv, v_cbc)
                        v_uc2 = k.op(DVE, lambda e, b=bgU[q], q=q, sl=sl: e.tensor_tensor(
                            out=Uc[:, sl * 512:(sl + 1) * 512], in0=pbs[b][:, :],
                            in1=comb_bc[:, q * 512:(q + 1) * 512], op=ALU.mult), s_dv, 1)
                    else:
                        v_uc2 = k.op(DVE, lambda e, b=bgU[q], sl=sl: e.tensor_copy(
                            out=Uc[:, sl * 512:(sl + 1) * 512], in_=pbs[b][:, :]), s_dv, 1)
                    k.wait(POOL, s_ac, v_sg)
                    k.wait(POOL, s_dv, v_uc2)
                    v_hs = k.op(POOL, lambda e, q=q, sl=sl, ic=ic, Hc=Hcur: e.tensor_tensor(
                        out=Hc[:, ic * T + q * 512: ic * T + (q + 1) * 512],
                        in0=sG[:, q * 512:(q + 1) * 512], in1=Uc[:, sl * 512:(sl + 1) * 512],
                        op=ALU.mult), s_po, 1)
                    sguc_free[q] = (s_po, v_hs)
                    uc_free[sl] = v_hs
                    release_bank(bgU[q], s_po, v_hs)
            if cons and PIPE_IL:
                emit_ed_evacs(emit_ed_pair(ic))
        if cons and not PIPE_IL:
            for ic in range(4):
                emit_ed_evacs(emit_ed_pair(ic))
        if prod:
            v_hs_of[ei] = v_hs
        if stage == "ffn_dbg" and ei == 2:
            k.op(DVE, lambda e: e.tensor_copy(out=dbg[:, 3 * 512:4 * 512], in_=comb_bc[:, 0:512]))
            k.op(DVE, lambda e: e.tensor_copy(out=dbg[:, 4 * 512:5 * 512], in_=Hs[:, 0:512]))

    if stage == "ffn_dbg":
        v_dbg = k.op(DVE, lambda e: e.engine_nop(), s_dv, 1)
        k.wait(SYNC, s_dv, v_dbg)
        k.op(SYNC, lambda e: e.dma_start(out=dbg_h[:, :], in_=dbg[:, :]), s_st, 16)
    v_st = None
    for c in range(NCH):
        k.wait(SYNC, s_dv, v_out_final[c])
        v_st = k.op(SYNC, lambda e, c=c: e.dma_start(
            out=out_h[c * 128:(c + 1) * 128, :], in_=x_res[:, c * T:(c + 1) * T]), s_st, 16)
    k.wait(SYNC, s_st, v_st)

    _play(nc, k)
    return nc


def _play(nc, k):
    with nc.Block() as block:
        def mk(name):
            def body(e):
                for fn in k.ops[name]:
                    fn(e)
            return body
        block.sync(mk(SYNC))
        block.tensor(mk(PE))
        block.vector(mk(DVE))
        block.scalar(mk(ACT))
        block.gpsimd(mk(POOL))


_NC_CACHE = {}


def _get_nc(stage="full"):
    if stage not in _NC_CACHE:
        _NC_CACHE[stage] = build_nc(stage)
    return _NC_CACHE[stage]


def _prep_inputs(inputs):
    f32 = np.float32
    x = np.asarray(inputs["L0_attn_x"], f32)
    wkT = np.ascontiguousarray(np.asarray(inputs["L0_attn_Wk"], f32).T)
    wvT = np.ascontiguousarray(np.asarray(inputs["L0_attn_Wv"], f32).T)
    wrT = np.ascontiguousarray(np.asarray(inputs["L0_attn_Wr"], f32).T)
    woT = np.ascontiguousarray(np.asarray(inputs["L0_attn_Wo"], f32).T)
    gateT = np.ascontiguousarray(np.asarray(inputs["L0_ffn_gate_w"], f32).T)
    gbias = np.ascontiguousarray(np.asarray(inputs["L0_ffn_gate_bias"], f32).reshape(1, 64))
    vecs = np.ascontiguousarray(np.concatenate(
        [np.asarray(inputs[k], f32).reshape(8, 128).T for k in
         ["L0_attn_time_decay", "L0_attn_time_first", "L0_attn_time_mix_k",
          "L0_attn_time_mix_v", "L0_attn_time_mix_r"]], axis=1))

    eg = np.asarray(inputs["L0_ffn_eg_w"], f32)
    eu = np.asarray(inputs["L0_ffn_eu_w"], f32)
    ed = np.asarray(inputs["L0_ffn_ed_w"], f32)
    sgT = np.asarray(inputs["L0_ffn_sg_w"], f32).T
    suT = np.asarray(inputs["L0_ffn_su_w"], f32).T
    sdT = np.asarray(inputs["L0_ffn_sd_w"], f32).T

    ew = np.empty((NEXP, EXP_SZ), f32)
    for h in range(2):
        ew[h, 0:EG_SZ] = np.ascontiguousarray(sgT[:, h * 512:(h + 1) * 512]).reshape(-1)
        ew[h, EG_SZ:2 * EG_SZ] = np.ascontiguousarray(suT[:, h * 512:(h + 1) * 512]).reshape(-1)
        ew[h, 2 * EG_SZ:] = np.ascontiguousarray(sdT[h * 512:(h + 1) * 512, :]).reshape(-1)
    egT = np.transpose(eg, (0, 2, 1))
    euT = np.transpose(eu, (0, 2, 1))
    edT = np.transpose(ed, (0, 2, 1))
    for e in range(NE):
        ew[2 + e, 0:EG_SZ] = np.ascontiguousarray(egT[e]).reshape(-1)
        ew[2 + e, EG_SZ:2 * EG_SZ] = np.ascontiguousarray(euT[e]).reshape(-1)
        ew[2 + e, 2 * EG_SZ:] = np.ascontiguousarray(edT[e]).reshape(-1)
    ew = np.ascontiguousarray(ew.reshape(-1))

    shared = dict(wkT=wkT, wvT=wvT, wrT=wrT, woT=woT, gateT=gateT, gbias=gbias,
                  vecs=vecs, EW=ew)
    in_maps = []
    for b in range(8):
        m = dict(shared)
        m["xT"] = np.ascontiguousarray(x[b].T)
        in_maps.append(m)
    return in_maps


def kernel(**inputs):
    in_maps = _prep_inputs(inputs)
    nc = _get_nc("full")
    res = run_bass_kernel_spmd(nc, in_maps, core_ids=list(range(8)))
    return np.stack([np.ascontiguousarray(r["outT"].T) for r in res.results], axis=0)
